# version 20
# baseline (speedup 1.0000x reference)
import numpy as np
import ml_dtypes
import concourse.bass as bass
import concourse.mybir as mybir
from concourse.bass_utils import run_bass_kernel_spmd

F32 = mybir.dt.float32
BF16 = mybir.dt.bfloat16
I32 = mybir.dt.int32
AF = mybir.ActivationFunctionType
OP = mybir.AluOpType
BF = ml_dtypes.bfloat16

S = 2
N = 2048
D = 1024
FF = 4096
NB = 4
QB = 512
NQB = N // QB
KC = 16
TB = 256
NTB = N // TB
FC = 32
EPS = 1e-5
EXPB = -12.0
VROW = KC * 8 * 130


def build_program(debug_spills=False):
    nc = bass.Bass("TRN2", target_bir_lowering=False)

    x_d = nc.dram_tensor("x", [S, N, D], F32, kind="ExternalInput")
    wqkv_d = nc.dram_tensor("w_qkvT", [D, 3 * D], F32, kind="ExternalInput")
    wproj_d = nc.dram_tensor("w_projT", [D, D], F32, kind="ExternalInput")
    bproj_d = nc.dram_tensor("b_proj", [D], F32, kind="ExternalInput")
    w1_d = nc.dram_tensor("w1", [D, FF], F32, kind="ExternalInput")
    b1_d = nc.dram_tensor("b1", [FF], F32, kind="ExternalInput")
    w2_d = nc.dram_tensor("w2", [FF, D], F32, kind="ExternalInput")
    b2_d = nc.dram_tensor("b2", [D], F32, kind="ExternalInput")
    y_d = nc.dram_tensor("y", [S, N, D], F32, kind="ExternalOutput")
    spk = "ExternalOutput" if debug_spills else "Internal"
    xnsp_d = nc.dram_tensor("xn_sp", [S, 128, 8, N], BF16, kind=spk)
    qsp_d = nc.dram_tensor("q_sp", [S, 128, 8, N], BF16, kind=spk)
    p1sp_d = nc.dram_tensor("p1_sp", [S, 16, 128, D], F32, kind=spk)

    ident_d = nc.inline_tensor(
        np.ascontiguousarray(np.eye(128, dtype=np.float32).astype(BF)), name="identu")
    sel_np = np.zeros((16, 8 * 128), dtype=np.float32)
    for hp in range(8):
        sel_np[2 * hp, hp * 128: hp * 128 + 64] = 1.0
        sel_np[2 * hp + 1, hp * 128 + 64: hp * 128 + 128] = 1.0
    sel_d = nc.inline_tensor(np.ascontiguousarray(sel_np.astype(BF)), name="selu")
    ones_d = nc.inline_tensor(
        np.ascontiguousarray(np.ones((1, 512), np.float32).astype(BF)), name="onesu")
    CONSTS = [0.125, EXPB, 256.0, 0.25, 1.0]
    consts_d = nc.inline_tensor(np.tile(np.array(CONSTS, np.float32)[None, :], (128, 1)), name="constsf")

    SBUF_LO = 16 * 1024 + 512
    SBUF_HI = 229344 - 64
    cur = [SBUF_LO]

    def sa(name, shape, dtype):
        per = int(np.prod(shape[1:])) * mybir.dt.size(dtype)
        at = (cur[0] + 31) & ~31
        cur[0] = at + per
        assert cur[0] <= SBUF_HI, f"SBUF overflow at {name}: {cur[0]}"
        return nc.alloc_sbuf_tensor_at(name, list(shape), dtype, offset=at)

    consts = sa("consts", [128, len(CONSTS)], F32)
    ident = sa("ident", [128, 128], BF16)
    sel = sa("sel", [16, 8 * 128], BF16)
    ones_bf = sa("ones_bf", [1, 512], BF16)
    b1c = sa("b1c", [128, FC], F32)
    b2_bf = sa("b2_bf", [1, D], BF16)
    bproj_bf = sa("bproj_bf", [1, D], BF16)
    b1st = sa("b1st", [128, FC], F32)
    ident32 = sa("ident32", [128, 128], F32)
    PERSIST_END = cur[0]

    for i, v in enumerate(CONSTS):
        nc.const_aps.register(F32, v, consts.ap()[:, i:i + 1])

    def region(start):
        saved = [start]

        def alloc(name, shape, dtype):
            cur[0] = saved[0]
            t = sa(name, shape, dtype)
            saved[0] = cur[0]
            return t
        return alloc, saved

    saAB, endAB = region(PERSIST_END)
    K_sb = saAB("K_sb", [128, 8 * N], BF16)
    vt = saAB("vt", [128, VROW], BF16)
    AB_END = endAB[0]

    saA, endA = region(AB_END)
    wqkvT = saA("wqkvT", [128, 8 * 3072], BF16)
    xn_fm = saA("xn_fm", [128, 8 * N], BF16)
    xbuf = saA("xbuf", [128, 4 * D], F32)
    qstage = saA("qstage", [128, 8 * 512], BF16)
    xnstage = saA("xnstage", [128, 8 * 512], BF16)
    vstage = saA("vstage", [128, 2 * 512], BF16)
    xn_tm = saA("xn_tm", [128, 2 * D], BF16)
    st12 = saA("st12", [128, 4 * 12], F32)
    mv8 = saA("mv8", [128, 8], F32)
    lnb = saA("lnb", [128, 8 * 4], F32)

    saB, endB = region(AB_END)
    wprojT = saB("wprojT", [128, 8 * D], BF16)
    qblk = saB("qblk", [128, 2 * 8 * QB], BF16)
    pT = saB("pT", [128, 2 * KC * QB], BF16)
    attn_un = saB("attn_un", [128, 8 * QB], BF16)
    attn_fm = saB("attn_fm", [128, 8 * QB], BF16)
    ptmp = saB("ptmp", [64, 2 * QB], BF16)
    denstage = saB("denstage", [128, 2 * QB], F32)
    den16 = saB("den16", [16, QB], F32)
    rden_bf = saB("rden_bf", [16, QB], BF16)
    rdbc = saB("rdbc", [128, QB], F32)
    p1stage = saB("p1stage", [128, D], F32)
    mpart = saB("mpart", [128, 64], F32)
    m128 = saB("m128", [128, 16], F32)
    mneg = saB("mneg", [128, 2 * 4 * 65], F32)
    mrow2 = saB("mrow2", [65, 2 * 16 * QB], BF16)
    onesp = saB("onesp", [65, 128], BF16)

    saD, endD = region(PERSIST_END)
    w1_sb = saD("w1_sb", [128, 8 * FF], BF16)
    w2_sb = saD("w2_sb", [128, FC * D], BF16)
    xnblk = saD("xnblk", [128, 2 * 8 * TB], BF16)
    h2 = saD("h2", [128, 3 * 512], BF16)
    g_ki = saD("g_ki", [128, 2 * 512], I32)
    g_v = saD("g_v", [128, 512], F32)
    g_ji = saD("g_ji", [128, 512], I32)
    g_g = saD("g_g", [128, 2 * 512], F32)
    g_yj = saD("g_yj", [128, 2 * 512], F32)
    g_yj1 = saD("g_yj1", [128, 2 * 512], F32)
    g_f = saD("g_f", [128, 2 * 512], F32)
    g_d = saD("g_d", [128, 512], F32)
    g_y = saD("g_y", [128, 512], F32)
    g_yi = saD("g_yi", [128, 2 * 512], I32)
    yi2 = saD("yi2", [128, 2 * 512], I32)
    xfin = saD("xfin", [128, 2 * D], F32)
    p1fin = saD("p1fin", [128, 2 * D], F32)
    outstage = saD("outstage", [128, D], F32)
    assert max(endA[0], endB[0], endD[0]) <= SBUF_HI, (endA[0], endB[0], endD[0])
    bstage2 = nc.alloc_sbuf_tensor_at("bstage2", [1, 2 * D], F32, offset=(PERSIST_END + 31) & ~31)

    ENGS = ("sync", "tensor", "scalar", "vector", "gpsimd")
    prog = {e: [] for e in ENGS}
    cnt = {"ld": 0, "st": 0, "pe": 0, "act": 0, "dve": 0}
    hwm = {e: {s: 0 for s in cnt} for e in ENGS}
    sems = {}

    def wait(e, sem, val):
        if val is None or val <= 0 or hwm[e][sem] >= val:
            return
        hwm[e][sem] = val
        prog[e].append(lambda eng, s=sem, v=val: eng.wait_ge(sems[s], v))

    def ins(e, fn, sem=None, n=1):
        if sem is None:
            prog[e].append(fn)
            return None
        cnt[sem] += n
        v = cnt[sem]
        prog[e].append(lambda eng, f=fn, s=sem, k=n: f(eng).then_inc(sems[s], k))
        return v

    def barrier():
        snap = dict(cnt)
        for e in ENGS:
            for s, v in snap.items():
                wait(e, s, v)

    def setup():
        ins("sync", lambda e: e.dma_start(consts.ap()[:, :], consts_d.ap()[:, :]), "ld", 16)
        ins("sync", lambda e: e.dma_start(ident.ap()[:, :], ident_d.ap()[:, :]), "ld", 16)
        ins("sync", lambda e: e.dma_start(sel.ap()[:, :], sel_d.ap()[:, :]), "ld", 16)
        ins("sync", lambda e: e.dma_start(ones_bf.ap()[:, :], ones_d.ap()[:, :]), "ld", 16)
        ins("sync", lambda e: e.dma_start(b1st.ap()[:, :], b1_d.ap().rearrange("(a b) -> b a", b=128),
                                          allow_slow_non_contiguous=True), "ld", 16)
        ins("sync", lambda e: e.dma_start(bstage2.ap()[0:1, 0:D], b2_d.ap().rearrange("(o a) -> o a", o=1)), "ld", 16)
        v = ins("sync", lambda e: e.dma_start(bstage2.ap()[0:1, D:2 * D], bproj_d.ap().rearrange("(o a) -> o a", o=1)), "ld", 16)
        wait("vector", "ld", v)
        ins("vector", lambda e: e.tensor_scalar(b1c.ap()[:, :], b1st.ap()[:, :], 256.0, None, OP.mult), "dve")
        ins("vector", lambda e: e.tensor_copy(ident32.ap()[:, :], ident.ap()[:, :]), "dve")
        ins("vector", lambda e: e.tensor_copy(b2_bf.ap()[0:1, :], bstage2.ap()[0:1, 0:D]), "dve")
        return ins("vector", lambda e: e.tensor_copy(bproj_bf.ap()[0:1, :], bstage2.ap()[0:1, D:2 * D]), "dve")

    def prep_wT(nchunks, src_d, stg, bf, dst, dst_stride, ps_tr, nbuf=2):
        stcast, trdone, evdone = {}, {}, {}
        dview = dst.ap().rearrange("p (dc q) -> p dc q", dc=8)
        pview = ps_tr.ap().rearrange("p (dc t) -> p dc t", dc=8)
        for c in range(nchunks):
            b = c % nbuf
            if c >= nbuf:
                wait("sync", "dve", stcast[b])
            v = ins("sync", lambda e, c=c, b=b: e.dma_start(
                stg.ap()[:, b * D:(b + 1) * D], src_d.ap()[c * 128:(c + 1) * 128, :]), "ld", 16)
            wait("vector", "ld", v)
            if c >= nbuf:
                wait("vector", "pe", trdone[b])
            stcast[b] = ins("vector", lambda e, b=b: e.tensor_copy(
                bf.ap()[:, b * D:(b + 1) * D], stg.ap()[:, b * D:(b + 1) * D]), "dve")
            wait("tensor", "dve", stcast[b])
            if c >= 1:
                wait("tensor", "dve", evdone[(c - 1) % nbuf])
            for dc in range(8):
                ins("tensor", lambda e, b=b, dc=dc: e.transpose(
                    ps_tr.ap()[:, dc * 128:(dc + 1) * 128],
                    bf.ap()[:, b * D + dc * 128: b * D + (dc + 1) * 128],
                    ident.ap()[:, :]), None)
            trdone[b] = ins("tensor", lambda e: e.drain(), "pe")
            wait("vector", "pe", trdone[b])
            evdone[b] = ins("vector", lambda e, c=c: e.tensor_copy(
                dview[:, :, c * 128:(c + 1) * 128], pview), "dve")
        return cnt["dve"]

    def phase_A(s, wq_ready, ps_qkv, ps_tr, ps_vt):
        xn_ready = {}
        xlast = {}
        xn_t_done = {}
        xnspill = {}
        qev, kvev = {}, {}
        vtr, vev = {}, {}
        qspill_st = None
        xn_fm_v = xn_fm.ap().rearrange("p (dc t) -> p dc t", dc=8)
        xns_v = xnstage.ap().rearrange("p (dc t) -> p dc t", dc=8)
        ptr_v = ps_tr.ap().rearrange("p (dc t) -> p dc t", dc=8)
        mvv = mv8.ap().rearrange("p (c two) -> p c two", two=2)
        for tb in range(NB):
            for c in range(4):
                tc = 4 * tb + c
                if tb > 0:
                    wait("sync", "act", xlast[tc - 4])
                v = ins("sync", lambda e, s=s, tc=tc, c=c: e.dma_start(
                    xbuf.ap()[:, c * D:(c + 1) * D],
                    x_d.ap()[s, tc * 128:(tc + 1) * 128, :]), "ld", 16)
                wait("vector", "ld", v)
                if tb > 0 and c == 0:
                    wait("vector", "act", xlast[tc - 1])
                o = c * 12
                ins("vector", lambda e, c=c, o=o: e.bn_stats(st12.ap()[:, o:o + 6], xbuf.ap()[:, c * D:c * D + 512]))
                ins("vector", lambda e, c=c, o=o: e.bn_stats(st12.ap()[:, o + 6:o + 12], xbuf.ap()[:, c * D + 512:c * D + 1024]))
            for c in range(4):
                ins("vector", lambda e, c=c: e.bn_aggr(mv8.ap()[:, 2 * c:2 * c + 2], st12.ap()[:, 12 * c:12 * c + 12]))
            ins("vector", lambda e: e.drain())
            sv = ins("vector", lambda e: e.tensor_scalar(lnb.ap()[:, 0:4], mvv[:, :, 1], EPS, None, OP.add), "dve")
            wait("scalar", "dve", sv)
            av = ins("scalar", lambda e: e.activation(lnb.ap()[:, 4:8], lnb.ap()[:, 0:4], AF.Sqrt), "act")
            wait("vector", "act", av)
            ins("vector", lambda e: e.reciprocal(lnb.ap()[:, 8:12], lnb.ap()[:, 4:8]))
            ins("vector", lambda e: e.drain())
            ins("vector", lambda e: e.tensor_mul(lnb.ap()[:, 12:16], lnb.ap()[:, 0:4], lnb.ap()[:, 8:12]))
            ins("vector", lambda e: e.drain())
            ins("vector", lambda e: e.tensor_mul(lnb.ap()[:, 16:20], lnb.ap()[:, 12:16], lnb.ap()[:, 8:12]))
            ins("vector", lambda e: e.drain())
            ins("vector", lambda e: e.tensor_scalar(lnb.ap()[:, 20:24], lnb.ap()[:, 16:20], -0.5, 1.5, OP.mult, OP.add))
            ins("vector", lambda e: e.drain())
            ins("vector", lambda e: e.tensor_mul(lnb.ap()[:, 24:28], lnb.ap()[:, 8:12], lnb.ap()[:, 20:24]))
            ins("vector", lambda e: e.drain())
            nv = ins("vector", lambda e: e.scalar_tensor_tensor(
                lnb.ap()[:, 28:32], mvv[:, :, 0], -1.0, lnb.ap()[:, 24:28], OP.mult, OP.mult), "dve")
            wait("scalar", "dve", nv)
            for c in range(4):
                tc = 4 * tb + c
                par = tc % 2
                wait("scalar", "pe", xn_t_done.get(par))
                xlast[tc] = ins("scalar", lambda e, c=c, par=par: e.activation(
                    xn_tm.ap()[:, par * D:(par + 1) * D], xbuf.ap()[:, c * D:(c + 1) * D],
                    AF.Identity, bias=lnb.ap()[:, 28 + c:29 + c], scale=lnb.ap()[:, 24 + c:25 + c]), "act")
                wait("tensor", "act", xlast[tc])
                wait("tensor", "st" if tc == 0 else "dve", wq_ready if tc == 0 else xn_ready[tc - 1])
                for dc in range(8):
                    ins("tensor", lambda e, par=par, dc=dc: e.transpose(
                        ps_tr.ap()[:, dc * 128:(dc + 1) * 128],
                        xn_tm.ap()[:, par * D + dc * 128: par * D + (dc + 1) * 128],
                        ident.ap()[:, :]), None)
                xn_t_done[par] = ins("tensor", lambda e: e.drain(), "pe")
                wait("vector", "pe", xn_t_done[par])
                if tb > 0 and c == 0:
                    wait("vector", "st", xnspill[tb - 1])
                ins("vector", lambda e, tc=tc: e.tensor_copy(xn_fm_v[:, :, tc * 128:(tc + 1) * 128], ptr_v))
                xn_ready[tc] = ins("vector", lambda e, c=c: e.tensor_copy(
                    xns_v[:, :, c * 128:(c + 1) * 128], ptr_v), "dve")
            wait("gpsimd", "dve", xn_ready[4 * tb + 3])
            xnspill[tb] = ins("gpsimd", lambda e, s=s, tb=tb: e.dma_start(
                xnsp_d.ap()[s, :, :, tb * 512:(tb + 1) * 512], xns_v), "st", 16)

            wait("tensor", "dve", xn_ready[4 * tb + 3])
            if tb == 0:
                wait("tensor", "st", wq_ready)
            for fc24 in range(24):
                bank = fc24 % 3
                prev = fc24 - 3 if fc24 >= 3 else (fc24 + 21 if tb > 0 else None)
                if prev is not None:
                    if prev < 8:
                        wait("tensor", "act", qev[prev])
                    else:
                        wait("tensor", "dve", kvev[prev])
                for dc in range(8):
                    ins("tensor", lambda e, fc24=fc24, dc=dc, tb=tb, bank=bank: e.matmul(
                        ps_qkv[bank].ap()[:, :],
                        wqkvT.ap()[:, dc * 3072 + fc24 * 128: dc * 3072 + (fc24 + 1) * 128],
                        xn_fm.ap()[:, dc * 2048 + tb * 512: dc * 2048 + (tb + 1) * 512],
                        start=(dc == 0), stop=(dc == 7)), None)
                mm = ins("tensor", lambda e: e.drain(), "pe")
                if fc24 < 8:
                    wait("scalar", "pe", mm)
                    if tb > 0 and fc24 == 0:
                        wait("scalar", "st", qspill_st)
                    qev[fc24] = ins("scalar", lambda e, fc24=fc24, bank=bank: e.activation(
                        qstage.ap()[:, fc24 * 512:(fc24 + 1) * 512],
                        ps_qkv[bank].ap()[:, :], AF.Copy, scale=0.125), "act")
                elif fc24 < 16:
                    hp = fc24 - 8
                    wait("vector", "pe", mm)
                    kvev[fc24] = ins("vector", lambda e, hp=hp, tb=tb, bank=bank: e.tensor_copy(
                        K_sb.ap()[:, hp * 2048 + tb * 512: hp * 2048 + (tb + 1) * 512],
                        ps_qkv[bank].ap()[:, :]), "dve")
                else:
                    hp = fc24 - 16
                    vb = hp % 2
                    wait("vector", "pe", mm)
                    wait("vector", "pe", vtr.get(vb))
                    kvev[fc24] = ins("vector", lambda e, vb=vb, bank=bank: e.tensor_copy(
                        vstage.ap()[:, vb * 512:(vb + 1) * 512], ps_qkv[bank].ap()[:, :]), "dve")
                    wait("tensor", "dve", kvev[fc24])
                    wait("tensor", "dve", vev.get(1 - vb))
                    for j in range(4):
                        ins("tensor", lambda e, vb=vb, j=j: e.transpose(
                            ps_vt.ap()[:, j * 128:(j + 1) * 128],
                            vstage.ap()[:, vb * 512 + j * 128: vb * 512 + (j + 1) * 128],
                            ident.ap()[:, :]), None)
                    vtr[vb] = ins("tensor", lambda e: e.drain(), "pe")
                    wait("vector", "pe", vtr[vb])
                    vev[vb] = ins("vector", lambda e, tb=tb, hp=hp: e.tensor_copy(
                        bass.AP(vt, (4 * tb) * 1040 + hp * 130, [[VROW, 128], [1040, 4], [65, 2], [1, 64]]),
                        bass.AP(ps_vt, 0, [[512, 128], [128, 4], [64, 2], [1, 64]])), "dve")
            wait("gpsimd", "act", qev[7])
            qspill_st = ins("gpsimd", lambda e, s=s, tb=tb: e.dma_start(
                qsp_d.ap()[s, :, :, tb * 512:(tb + 1) * 512],
                qstage.ap().rearrange("p (qc t) -> p qc t", qc=8)), "st", 16)
        ins("vector", lambda e: e.memset(bass.AP(vt, 64, [[VROW, 128], [130, 128], [1, 1]]), 1.0), "dve")
        ov = ins("vector", lambda e: e.memset(bass.AP(vt, 129, [[VROW, 128], [130, 128], [1, 1]]), 1.0), "dve")
        return ov, qspill_st, xnspill[NB - 1]

    def phase_B(s, vt_ready, qsp_ready, ps_sx, ps_tm, ps_rd, ps_proj, ps_m):
        qblk_v = qblk.ap().rearrange("p (b qc t) -> p b qc t", b=2, qc=8)
        qld = {}
        exp_prev = pv_done = pv_evac = rdevac = pevac = pspill = None
        pspill2 = {}
        normmul = {}
        mxfer = {}
        tmz_last = {}
        tmred = {}
        tmred_last = [None]
        exp_hist, gkc = {}, [0]
        tmb = [ps_tm[0], ps_tm[1], ps_rd, ps_m]
        ins("vector", lambda e: e.memset(onesp.ap()[:, :], 1.0), "dve")

        def tm_mm(qbn, hp, m, pe_guard):
            kc, par = m // 2, m % 2
            q128, kgr = kc // 4, kc % 4
            qbb, bank = qbn % 2, tmb[m % 4]
            p0 = 64 * par
            wait("tensor", "dve", tmred.get((qbn, hp, m - 4)))
            if m % 4 == 2:
                wait("tensor", "dve", rdevac)
            if m % 4 == 3:
                wait("tensor", "dve", tmz_last.get("evac"))
            ins("tensor", lambda e, hp=hp, q128=q128, kgr=kgr, p0=p0, qbb=qbb, bank=bank: e.matmul(
                bank.ap()[:, :],
                qblk.ap()[p0:p0 + 64, qbb * (8 * QB) + hp * QB + q128 * 128: qbb * (8 * QB) + hp * QB + q128 * 128 + 128],
                K_sb.ap()[p0:p0 + 64, hp * 2048 + kgr * 512: hp * 2048 + (kgr + 1) * 512],
                start=True, stop=True, tile_position=(p0, 0)), None)

        def tm_red(qbn, hp, m, pe_guard):
            kc, par = m // 2, m % 2
            q128, kgr = kc // 4, kc % 4
            par_q = qbn % 2
            bank = tmb[m % 4]
            col = par * 16 + q128 * 4 + kgr
            wait("vector", "pe", pe_guard)
            r = ins("vector", lambda e, col=col, par_q=par_q, bank=bank: e.reduce_max(
                mpart.ap()[:, par_q * 32 + col:par_q * 32 + col + 1],
                bank.ap()[:, :], mybir.AxisListType.X), "dve")
            tmred[(qbn, hp, m)] = r
            tmred_last[0] = r

        def tm_combine(qbn, hp):
            par_q = qbn % 2
            ins("vector", lambda e: e.drain())
            for par in range(2):
                for q128 in range(4):
                    o = par_q * 32 + par * 16 + q128 * 4
                    ins("vector", lambda e, o=o, par_q=par_q, par=par, q128=q128: e.reduce_max(
                        m128.ap()[:, par_q * 8 + 4 * par + q128: par_q * 8 + 4 * par + q128 + 1],
                        mpart.ap()[:, o:o + 4], mybir.AxisListType.X))
            ins("vector", lambda e: e.drain())
            mz = ins("vector", lambda e, par_q=par_q: e.tensor_scalar(
                bass.AP(mneg, par_q * (4 * 65), [[2 * 4 * 65, 128], [64, 2], [65, 4]]),
                bass.AP(m128, par_q * 8, [[16, 128], [4, 2], [1, 4]]),
                -1.0, None, OP.mult), "dve")
            wait("tensor", "dve", mz)
            wait("tensor", "dve", tmz_last.get("evac"))
            for q128 in range(4):
                ins("tensor", lambda e, q128=q128, par_q=par_q: e.transpose(
                    ps_m.ap()[0:65, q128 * 128:(q128 + 1) * 128],
                    mneg.ap()[:, par_q * (4 * 65) + 65 * q128: par_q * (4 * 65) + 65 * q128 + 65],
                    ident32.ap()[:, :]), None)
            tmz = ins("tensor", lambda e: e.drain(), "pe")
            wait("vector", "pe", tmz)
            ev = None
            for q128 in range(4):
                for par in range(2):
                    h = 2 * hp + par
                    ev = ins("vector", lambda e, par=par, h=h, q128=q128, par_q=par_q: e.tensor_copy(
                        mrow2.ap()[64 * par:64 * par + 1, par_q * (16 * QB) + h * QB + q128 * 128: par_q * (16 * QB) + h * QB + q128 * 128 + 128],
                        ps_m.ap()[64 * par:64 * par + 1, q128 * 128:(q128 + 1) * 128]), "dve")
            mxfer[(qbn, hp)] = ev
            tmz_last["evac"] = ev

        wait("sync", "st", qsp_ready)
        for pf in range(2):
            qld[pf] = ins("sync", lambda e, s=s, pf=pf: e.dma_start(
                qblk_v[:, pf], qsp_d.ap()[s, :, :, pf * QB:(pf + 1) * QB]), "ld", 16)
        wait("tensor", "ld", qld[0])
        wait("tensor", "dve", vt_ready)
        for hp in range(8):
            for kc in range(KC):
                for j in range(2):
                    tm_mm(0, hp, 2 * kc + j, None)
                dr = ins("tensor", lambda e: e.drain(), "pe")
                for j in range(2):
                    tm_red(0, hp, 2 * kc + j, dr)
            tm_combine(0, hp)

        for qb in range(NQB):
            qbb = qb % 2
            if 1 <= qb < NQB - 1:
                wait("sync", "pe", scores_last_of_prev)
                qld[(qb + 1) % 2] = ins("sync", lambda e, s=s, qb=qb: e.dma_start(
                    qblk_v[:, (qb + 1) % 2], qsp_d.ap()[s, :, :, (qb + 1) * QB:(qb + 2) * QB]), "ld", 16)
            wait("tensor", "ld", qld[qbb])
            den_dma, move_dma = {}, {}
            for hp in range(8):
                exp_done = None
                scores_done = None
                wait("tensor", "dve", mxfer[(qb, hp)])
                if qb + 1 < NQB and hp == 0:
                    wait("tensor", "ld", qld[(qb + 1) % 2])
                for kcp in range(KC // 2):
                    for kc in (2 * kcp, 2 * kcp + 1):
                        sb = ps_sx if kc % 2 == 0 else ps_proj
                        wait("tensor", "act", exp_hist.get(gkc[0] + (kc % 2) - 2))
                        if kc == 0 and hp == 0:
                            wait("tensor", "dve", rdevac)
                        if kc == 1:
                            wait("tensor", "dve", pv_evac)
                            if qb > 0 and hp == 0:
                                wait("tensor", "dve", pevac)
                        for par in range(2):
                            p0 = 64 * par
                            ins("tensor", lambda e, hp=hp, kc=kc, par=par, p0=p0, qbb=qbb, sb=sb: e.matmul(
                                sb.ap()[:, par * 512:(par + 1) * 512],
                                K_sb.ap()[p0:p0 + 64, hp * 2048 + kc * 128: hp * 2048 + (kc + 1) * 128],
                                qblk.ap()[p0:p0 + 64, qbb * (8 * QB) + hp * QB: qbb * (8 * QB) + (hp + 1) * QB],
                                start=True, stop=False, tile_position=(p0, 0)),
                                None)
                        for par in range(2):
                            h = 2 * hp + par
                            ins("tensor", lambda e, par=par, h=h, qbb=qbb, sb=sb: e.matmul(
                                sb.ap()[:, par * 512:(par + 1) * 512],
                                onesp.ap()[64 * par:64 * par + 1, 0:128],
                                mrow2.ap()[64 * par:64 * par + 1, qbb * (16 * QB) + h * QB: qbb * (16 * QB) + (h + 1) * QB],
                                start=False, stop=True, tile_position=(64 * par, 0)),
                                None)
                        if qb + 1 < NQB:
                            for j in range(2):
                                tm_mm(qb + 1, hp, 2 * kc + j, None)
                    scores_done = ins("tensor", lambda e: e.drain(), "pe")
                    if qb + 1 < NQB:
                        for kk in (2 * kcp, 2 * kcp + 1):
                            for j in range(2):
                                tm_red(qb + 1, hp, 2 * kk + j, scores_done)
                    wait("scalar", "pe", scores_done)
                    if kcp == 0:
                        wait("scalar", "pe", pv_done)
                    for kc in (2 * kcp, 2 * kcp + 1):
                        sb = ps_sx if kc % 2 == 0 else ps_proj
                        ev = ins("scalar", lambda e, kc=kc, sb=sb: e.activation(
                            bass.AP(pT, kc * QB, [[2 * KC * QB, 128], [KC * QB, 2], [1, QB]]),
                            sb.ap()[:, :], AF.Exp), "act")
                        exp_hist[gkc[0]] = ev
                        gkc[0] += 1
                        exp_prev = ev
                        exp_done = ev
                scores_last_of_prev = scores_done
                if qb + 1 < NQB:
                    tm_combine(qb + 1, hp)
                wait("tensor", "act", exp_done)
                wait("tensor", "dve", pv_evac)
                if qb > 0 and hp == 0:
                    wait("tensor", "dve", pevac)
                for par in range(2):
                    for kc in range(KC):
                        ins("tensor", lambda e, hp=hp, par=par, kc=kc: e.matmul(
                            ps_proj.ap()[0:65, par * 512:(par + 1) * 512],
                            vt.ap()[:, kc * 1040 + hp * 130 + par * 65: kc * 1040 + hp * 130 + par * 65 + 65],
                            pT.ap()[:, par * (KC * QB) + kc * QB: par * (KC * QB) + (kc + 1) * QB],
                            start=(kc == 0), stop=(kc == KC - 1)),
                            None)
                pv_done = ins("tensor", lambda e: e.drain(), "pe")
                wait("vector", "pe", pv_done)
                if hp >= 2:
                    wait("vector", "ld", move_dma[hp - 2])
                ins("vector", lambda e: e.tensor_copy(
                    denstage.ap()[64:65, 0:QB], ps_proj.ap()[64:65, 0:512]))
                ins("vector", lambda e: e.tensor_copy(
                    denstage.ap()[64:65, QB:2 * QB], ps_proj.ap()[64:65, 512:1024]))
                ins("vector", lambda e, hp=hp: e.tensor_copy(
                    attn_un.ap()[0:64, hp * QB:(hp + 1) * QB], ps_proj.ap()[0:64, 0:512]))
                pv_evac = ins("vector", lambda e, hp=hp: e.tensor_copy(
                    ptmp.ap()[0:64, (hp % 2) * QB:(hp % 2 + 1) * QB], ps_proj.ap()[0:64, 512:1024]), "dve")
                wait("sync", "dve", pv_evac)
                den_dma[hp] = ins("sync", lambda e, hp=hp: e.dma_start(
                    den16.ap()[2 * hp:2 * hp + 2, 0:QB], denstage.ap()[64:65, 0:2 * QB]), "ld", 16)
                move_dma[hp] = ins("sync", lambda e, hp=hp: e.dma_start(
                    attn_un.ap()[64:128, hp * QB:(hp + 1) * QB],
                    ptmp.ap()[0:64, (hp % 2) * QB:(hp % 2 + 1) * QB]), "ld", 16)
                wait("vector", "ld", den_dma[hp])
            wait("vector", "ld", den_dma[7])
            ins("vector", lambda e: e.reciprocal(den16.ap()[:, :], den16.ap()[:, :]))
            ins("vector", lambda e: e.drain())
            cv = ins("vector", lambda e: e.tensor_copy(rden_bf.ap()[:, :], den16.ap()[:, :]), "dve")
            wait("tensor", "dve", cv)
            wait("tensor", "dve", tmred_last[0])
            bankg = {0: ("dve", rdevac), 1: ("act", exp_done)}
            bmm = {}
            for hp in range(8):
                pb = ps_rd if hp % 2 == 0 else ps_sx
                wait("tensor", *bankg[hp % 2])
                ins("tensor", lambda e, hp=hp, pb=pb: e.matmul(
                    pb.ap()[:, 0:512], sel.ap()[:, hp * 128:(hp + 1) * 128], rden_bf.ap()[:, :],
                    start=True, stop=True), None)
                bmm[hp] = ins("tensor", lambda e: e.drain(), "pe")
                wait("vector", "pe", bmm[hp])
                rdevac = ins("vector", lambda e, pb=pb: e.tensor_copy(rdbc.ap()[:, :], pb.ap()[:, 0:512]), "dve")
                bankg[hp % 2] = ("dve", rdevac)
                ins("vector", lambda e: e.drain())
                wait("vector", "ld", move_dma[hp])
                normmul[hp] = ins("vector", lambda e, hp=hp: e.tensor_mul(
                    attn_fm.ap()[:, hp * QB:(hp + 1) * QB],
                    attn_un.ap()[:, hp * QB:(hp + 1) * QB], rdbc.ap()[:, :]), "dve")
            wait("tensor", "dve", normmul[7])
            for th in range(4):
                wait("tensor", "dve", pevac)
                for half in range(2):
                    ins("tensor", lambda e, half=half: e.matmul(
                        ps_proj.ap()[:, half * 512:(half + 1) * 512],
                        ones_bf.ap()[0:1, 0:128], bproj_bf.ap()[0:1, half * 512:(half + 1) * 512],
                        start=True, stop=False))
                    for dc in range(8):
                        ins("tensor", lambda e, half=half, dc=dc, th=th: e.matmul(
                            ps_proj.ap()[:, half * 512:(half + 1) * 512],
                            attn_fm.ap()[:, dc * QB + th * 128: dc * QB + th * 128 + 128],
                            wprojT.ap()[:, dc * D + half * 512: dc * D + (half + 1) * 512],
                            start=False, stop=(dc == 7)), None)
                pmm = ins("tensor", lambda e: e.drain(), "pe")
                wait("vector", "pe", pmm)
                wait("vector", "st", pspill)
                pevac = ins("vector", lambda e, th=th: e.tensor_copy(
                    p1stage.ap()[:, :], ps_proj.ap()[:, :]), "dve")
                wait("gpsimd", "dve", pevac)
                pspill = ins("gpsimd", lambda e, s=s, qb=qb, th=th: e.dma_start(
                    p1sp_d.ap()[s, 4 * qb + th], p1stage.ap()[:, :]), "st", 16)
        return pspill

    def phase_D(s, xnsp_ready, p1_ready, ps_c1, ps_c2):
        xnb_v = xnblk.ap().rearrange("p (b dc t) -> p b dc t", b=2, dc=8)
        wait("sync", "st", xnsp_ready)
        wait("sync", "st", p1_ready)

        w1q = {}
        w2c = {}
        wjobs = [("w1", q, 2 * q) for q in range(8)] + \
                [("w2", c, c // 2 + 4) for c in range(FC)]
        wjobs.sort(key=lambda j: j[2])
        for kind, idx, _need in wjobs:
            if kind == "w1":
                for c in range(8):
                    m = ins("gpsimd", lambda e, c=c, q=idx: e.dma_start(
                        w1_sb.ap()[:, c * FF + q * 512: c * FF + (q + 1) * 512],
                        w1_d.ap()[c * 128:(c + 1) * 128, q * 512:(q + 1) * 512]), "st", 16)
                w1q[idx] = m
            else:
                w2c[idx] = ins("gpsimd", lambda e, c=idx: e.dma_start(
                    w2_sb.ap()[:, c * D:(c + 1) * D],
                    w2_d.ap()[c * 128:(c + 1) * 128, :]), "st", 16)

        xnld = {}
        xnld[0] = ins("sync", lambda e, s=s: e.dma_start(xnb_v[:, 0], xnsp_d.ap()[s, :, :, 0:TB]), "ld", 16)
        xnld[1] = ins("sync", lambda e, s=s: e.dma_start(xnb_v[:, 1], xnsp_d.ap()[s, :, :, TB:2 * TB]), "ld", 16)

        NF = NTB * 16
        c1_reader, h2_reader = {}, {}
        s2_end, s4_end, hv_m, ki_m, yj1_m, ykv_m = {}, {}, {}, {}, {}, {}
        c1d_m, c2closed, xfm, pfm = {}, {}, {}, {}
        finadd = ki2v = ysent = None

        def ybuf(par):
            return (g_y if par == 0 else g_d).ap()

        for f in range(NF + 4):
            tb, fcp, gb = f // 16, f % 16, f % 2
            fC2 = f - 4
            if f < NF:
                if fcp == 0:
                    wait("tensor", "ld", xnld[tb % 2])
                wait("tensor", "act", c1_reader.get(gb))
                if tb == 0:
                    wait("tensor", "st", w1q[min((2 * fcp + 1) // 4 + 2, 7)])
                for i in range(2):
                    fc = 2 * fcp + i
                    for dc in range(8):
                        ins("tensor", lambda e, fc=fc, i=i, dc=dc, tbb=tb % 2, gb=gb: e.matmul(
                            ps_c1[gb].ap()[:, i * TB:(i + 1) * TB],
                            w1_sb.ap()[:, dc * FF + fc * 128: dc * FF + (fc + 1) * 128],
                            xnblk.ap()[:, tbb * (8 * TB) + dc * TB: tbb * (8 * TB) + (dc + 1) * TB],
                            start=(dc == 0), stop=(dc == 7)), None)
                c1d_m[f] = ins("tensor", lambda e: e.drain(), "pe")
            if 0 <= fC2 < NF:
                tb2, fcp2, sl2 = fC2 // 16, fC2 % 16, fC2 % 3
                wait("tensor", "dve", hv_m[fC2])
                if tb2 == 0:
                    wait("tensor", "st", w2c[min(2 * fcp2 + 8, FC - 1)])
                if fcp2 == 0:
                    wait("tensor", "act", ki2v)
                    for th in range(2):
                        for half in range(2):
                            ins("tensor", lambda e, th=th, half=half: e.matmul(
                                ps_c2[th].ap()[:, half * 512:(half + 1) * 512],
                                ones_bf.ap()[0:1, 0:128], b2_bf.ap()[0:1, half * 512:(half + 1) * 512],
                                start=True, stop=False))
                m = None
                for i in range(2):
                    fc = 2 * fcp2 + i
                    for th in range(2):
                        for half in range(2):
                            m = ins("tensor", lambda e, fc=fc, i=i, th=th, half=half, sl2=sl2: e.matmul(
                                ps_c2[th].ap()[:, half * 512:(half + 1) * 512],
                                h2.ap()[:, sl2 * 512 + i * TB + th * 128: sl2 * 512 + i * TB + th * 128 + 128],
                                w2_sb.ap()[:, fc * D + half * 512: fc * D + (half + 1) * 512],
                                start=False, stop=(fc == FC - 1)),
                                "pe" if (i, th, half) == (1, 1, 1) else None)
                h2_reader[sl2] = m
                if fcp2 == 15:
                    c2closed[tb2] = ins("tensor", lambda e: e.drain(), "pe")
            if f < NF:
                wait("scalar", "pe", c1d_m[f])
                wait("scalar", "dve", s2_end.get(f - 2))
                kv = None
                for i in range(2):
                    fc = 2 * fcp + i
                    kv = ins("scalar", lambda e, i=i, fc=fc, gb=gb: e.activation(
                        g_ki.ap()[:, gb * 512 + i * TB: gb * 512 + (i + 1) * TB],
                        ps_c1[gb].ap()[:, i * TB:(i + 1) * TB],
                        AF.Identity, bias=b1c.ap()[:, fc:fc + 1], scale=256.0), "act" if i == 1 else None)
                ki_m[f] = kv
                c1_reader[gb] = kv
            fy = f - 2
            if 0 <= fy < NF:
                gby = fy % 2
                wait("scalar", "dve", s4_end[fy])
                ykv_m[fy] = ins("scalar", lambda e, gby=gby, yb=ybuf(fy % 2): e.activation(
                    g_yi.ap()[:, gby * 512:(gby + 1) * 512], yb[:, :],
                    AF.Copy, scale=256.0), "act")
            fj = f - 1
            if 0 <= fj < NF:
                gbj = fj % 2
                wait("scalar", "dve", s2_end[fj])
                ins("scalar", lambda e, gbj=gbj: e.activation(
                    g_yj.ap()[:, gbj * 512:(gbj + 1) * 512],
                    g_g.ap()[:, gbj * 512:(gbj + 1) * 512], AF.Gelu, scale=0.25), None)
                yj1_m[fj] = ins("scalar", lambda e, gbj=gbj: e.activation(
                    g_yj1.ap()[:, gbj * 512:(gbj + 1) * 512],
                    g_g.ap()[:, gbj * 512:(gbj + 1) * 512], AF.Gelu, bias=0.25, scale=0.25), "act")
            if 0 <= fy < NF:
                gby = fy % 2
                sly = fy % 3
                wait("vector", "act", ykv_m[fy])
                wait("vector", "pe", h2_reader.get(sly))
                ins("vector", lambda e, gby=gby, sly=sly: e.tensor_scalar(
                    h2.ap()[:, sly * 512:(sly + 1) * 512], g_yi.ap()[:, gby * 512:(gby + 1) * 512],
                    1.0 / 256.0, None, OP.mult))
                hv_m[fy] = ins("vector", lambda e: e.drain(), "dve")
            if f < NF:
                wait("vector", "act", ki_m[f])
                ins("vector", lambda e, gb=gb: e.tensor_scalar(
                    g_v.ap()[:, :], g_ki.ap()[:, gb * 512:(gb + 1) * 512],
                    1.0 / 64.0, 15.5, OP.mult, OP.add))
                ins("vector", lambda e: e.tensor_copy(g_ji.ap()[:, :], g_v.ap()[:, :]))
                ins("vector", lambda e, gb=gb: e.tensor_scalar(
                    g_g.ap()[:, gb * 512:(gb + 1) * 512], g_ji.ap()[:, :],
                    127.0, 16.0, OP.min, OP.subtract))
                s2_end[f] = ins("vector", lambda e, gb=gb: e.scalar_tensor_tensor(
                    g_f.ap()[:, gb * 512:(gb + 1) * 512], g_v.ap()[:, :], 15.5,
                    g_g.ap()[:, gb * 512:(gb + 1) * 512], OP.subtract, OP.subtract), "dve")
            if 0 <= fj < NF:
                gbj = fj % 2
                wait("vector", "act", yj1_m[fj])
                yb = ybuf(gbj)
                ins("vector", lambda e, yb=yb, gbj=gbj: e.tensor_sub(
                    yb[:, :], g_yj1.ap()[:, gbj * 512:(gbj + 1) * 512],
                    g_yj.ap()[:, gbj * 512:(gbj + 1) * 512]))
                ins("vector", lambda e, yb=yb, gbj=gbj: e.tensor_mul(
                    yb[:, :], yb[:, :], g_f.ap()[:, gbj * 512:(gbj + 1) * 512]))
                s4_end[fj] = ins("vector", lambda e, yb=yb, gbj=gbj: e.tensor_add(
                    yb[:, :], yb[:, :], g_yj.ap()[:, gbj * 512:(gbj + 1) * 512]), "dve")
            if f < NF and fcp == 0 and 1 <= tb < NTB - 1:
                wait("sync", "pe", c1d_m[16 * tb - 1])
                xnld[(tb + 1) % 2] = ins("sync", lambda e, s=s, tb=tb: e.dma_start(
                    xnb_v[:, (tb + 1) % 2], xnsp_d.ap()[s, :, :, (tb + 1) * TB:(tb + 2) * TB]), "ld", 16)
            if f < NF and fcp == 4:
                wait("sync", "dve", finadd)
                xfm[tb] = ins("sync", lambda e, s=s, tb=tb: e.dma_start(
                    xfin.ap().rearrange("p (tc d) -> p tc d", tc=2),
                    x_d.ap()[s, 2 * tb * 128:(2 * tb + 2) * 128, :].rearrange("(tc p) d -> p tc d", p=128)), "ld", 16)
                pfm[tb] = ins("sync", lambda e, s=s, tb=tb: e.dma_start(
                    p1fin.ap().rearrange("p (tc d) -> p tc d", tc=2),
                    p1sp_d.ap()[s, 2 * tb:2 * tb + 2].rearrange("tc p d -> p tc d")), "ld", 16)
            if 0 <= fC2 < NF and fC2 % 16 == 15:
                tb2 = fC2 // 16
                wait("scalar", "pe", c2closed[tb2])
                for th in range(2):
                    wait("scalar", "dve", finadd)
                    ki2v = ins("scalar", lambda e, th=th: e.activation(
                        yi2.ap()[:, :], ps_c2[th].ap()[:, :], AF.Copy, scale=256.0), "act")
                    wait("vector", "act", ki2v)
                    wait("vector", "ld", xfm[tb2])
                    wait("vector", "ld", pfm[tb2])
                    wait("vector", "st", ysent)
                    ins("vector", lambda e, th=th: e.scalar_tensor_tensor(
                        outstage.ap()[:, :], yi2.ap()[:, :], 1.0 / 256.0,
                        xfin.ap()[:, th * D:(th + 1) * D], OP.mult, OP.add))
                    finadd = ins("vector", lambda e, th=th: e.tensor_add(
                        outstage.ap()[:, :], outstage.ap()[:, :],
                        p1fin.ap()[:, th * D:(th + 1) * D]), "dve")
                    wait("gpsimd", "dve", finadd)
                    ysent = ins("gpsimd", lambda e, s=s, tb2=tb2, th=th: e.dma_start(
                        y_d.ap()[s, (2 * tb2 + th) * 128:(2 * tb2 + th + 1) * 128, :],
                        outstage.ap()[:, :]), "st", 16)
        return cnt["st"]

    setup_dve = setup()

    gA = [nc.psum_tensor("ps_qkv%d_0" % i, [128, 512], F32) for i in range(3)]
    ps_qkv = [g.__enter__() for g in gA]
    gtr = nc.psum_tensor("ps_tr_0", [128, 1024], BF16)
    ps_tr = gtr.__enter__()
    gvt = nc.psum_tensor("ps_vt_0", [128, 512], BF16)
    ps_vt = gvt.__enter__()

    for s in range(S):
        barrier()
        for dc in range(8):
            wq_ready = ins("gpsimd", lambda e, dc=dc: e.dma_start(
                wqkvT.ap()[:, dc * 3072:(dc + 1) * 3072],
                wqkv_d.ap()[dc * 128:(dc + 1) * 128, :]), "st", 16)
        wait("vector", "dve", setup_dve)
        vt_ready, qsp_ready, xnsp_ready = phase_A(s, wq_ready, ps_qkv, ps_tr, ps_vt)
        barrier()
        for dc in range(8):
            wp_ready = ins("gpsimd", lambda e, dc=dc: e.dma_start(
                wprojT.ap()[:, dc * D:(dc + 1) * D],
                wproj_d.ap()[dc * 128:(dc + 1) * 128, :]), "st", 16)
        gvt.__exit__(None, None, None)
        gtr.__exit__(None, None, None)
        for g in reversed(gA):
            g.__exit__(None, None, None)
        gB = [nc.psum_tensor("ps_sx_%d" % s, [128, 1024], F32),
              nc.psum_tensor("ps_tm0_%d" % s, [128, 512], F32),
              nc.psum_tensor("ps_tm1_%d" % s, [128, 512], F32),
              nc.psum_tensor("ps_rd_%d" % s, [128, 512], F32),
              nc.psum_tensor("ps_proj_%d" % s, [128, 1024], F32),
              nc.psum_tensor("ps_m_%d" % s, [128, 512], F32)]
        ps_sx, ps_tm0, ps_tm1, ps_rd, ps_proj, ps_m = [g.__enter__() for g in gB]
        wait("tensor", "st", wp_ready)
        p1_ready = phase_B(s, vt_ready, qsp_ready, ps_sx, [ps_tm0, ps_tm1], ps_rd, ps_proj, ps_m)
        barrier()
        for g in reversed(gB):
            g.__exit__(None, None, None)
        gD = [nc.psum_tensor("ps_c1a_%d" % s, [128, 512], F32),
              nc.psum_tensor("ps_c1b_%d" % s, [128, 512], F32),
              nc.psum_tensor("ps_c2a_%d" % s, [128, 1024], F32),
              nc.psum_tensor("ps_c2b_%d" % s, [128, 1024], F32)]
        ps_c1a, ps_c1b, ps_c2a, ps_c2b = [g.__enter__() for g in gD]
        phase_D(s, xnsp_ready, p1_ready, [ps_c1a, ps_c1b], [ps_c2a, ps_c2b])
        for g in reversed(gD):
            g.__exit__(None, None, None)
        if s + 1 < S:
            barrier()
            gA = [nc.psum_tensor("ps_qkv%d_%d" % (i, s + 1), [128, 512], F32) for i in range(3)]
            ps_qkv = [g.__enter__() for g in gA]
            gtr = nc.psum_tensor("ps_tr_%d" % (s + 1), [128, 1024], BF16)
            ps_tr = gtr.__enter__()
            gvt = nc.psum_tensor("ps_vt_%d" % (s + 1), [128, 512], BF16)
            ps_vt = gvt.__enter__()

    with (
        nc.semaphore("ld") as sem_ld,
        nc.semaphore("st") as sem_st,
        nc.semaphore("pe") as sem_pe,
        nc.semaphore("act") as sem_act,
        nc.semaphore("dve") as sem_dve,
        nc.Block() as block,
    ):
        sems.update(ld=sem_ld, st=sem_st, pe=sem_pe, act=sem_act, dve=sem_dve)

        @block.sync
        def _(eng):
            for f in prog["sync"]:
                f(eng)

        @block.tensor
        def _(eng):
            for f in prog["tensor"]:
                f(eng)

        @block.scalar
        def _(eng):
            for f in prog["scalar"]:
                f(eng)

        @block.vector
        def _(eng):
            for f in prog["vector"]:
                f(eng)

        @block.gpsimd
        def _(eng):
            for f in prog["gpsimd"]:
                f(eng)
            eng.wait_ge(sem_st, cnt["st"])

    return nc


_cached = {}


def kernel(**inputs):
    x = np.ascontiguousarray(np.asarray(inputs["L0_attn_x"], dtype=np.float32))
    wqkv = np.ascontiguousarray(np.asarray(inputs["L0_attn_w_qkv"], dtype=np.float32))
    wproj = np.ascontiguousarray(np.asarray(inputs["L0_attn_w_proj"], dtype=np.float32))
    bproj = np.ascontiguousarray(np.asarray(inputs["L0_attn_b_proj"], dtype=np.float32))
    w1 = np.ascontiguousarray(np.asarray(inputs["L0_ffn_w1"], dtype=np.float32))
    b1 = np.ascontiguousarray(np.asarray(inputs["L0_ffn_b1"], dtype=np.float32))
    w2 = np.ascontiguousarray(np.asarray(inputs["L0_ffn_w2"], dtype=np.float32))
    b2 = np.ascontiguousarray(np.asarray(inputs["L0_ffn_b2"], dtype=np.float32))

    if "nc" not in _cached:
        _cached["nc"] = build_program()
    nc = _cached["nc"]
    in_maps = [{
        "x": np.ascontiguousarray(x[2 * c:2 * c + 2]),
        "w_qkvT": np.ascontiguousarray(wqkv.T), "w_projT": np.ascontiguousarray(wproj.T), "b_proj": bproj,
        "w1": w1, "b1": b1, "w2": w2, "b2": b2,
    } for c in range(8)]
    res = run_bass_kernel_spmd(nc, in_maps, core_ids=list(range(8)))
    return np.concatenate([r["y"] for r in res.results], axis=0)


# version 21
# speedup vs baseline: 1.0185x; 1.0185x over previous
import numpy as np
import ml_dtypes
import concourse.bass as bass
import concourse.mybir as mybir
from concourse.bass_utils import run_bass_kernel_spmd

F32 = mybir.dt.float32
BF16 = mybir.dt.bfloat16
I32 = mybir.dt.int32
AF = mybir.ActivationFunctionType
OP = mybir.AluOpType
BF = ml_dtypes.bfloat16

S = 2
N = 2048
D = 1024
FF = 4096
NB = 4
QB = 512
NQB = N // QB
KC = 16
TB = 256
NTB = N // TB
FC = 32
EPS = 1e-5
EXPB = -12.0
VROW = KC * 8 * 130


def build_program(debug_spills=False):
    nc = bass.Bass("TRN2", target_bir_lowering=False)

    x_d = nc.dram_tensor("x", [S, N, D], F32, kind="ExternalInput")
    wqkv_d = nc.dram_tensor("w_qkvT", [D, 3 * D], F32, kind="ExternalInput")
    wproj_d = nc.dram_tensor("w_projT", [D, D], F32, kind="ExternalInput")
    bproj_d = nc.dram_tensor("b_proj", [D], F32, kind="ExternalInput")
    w1_d = nc.dram_tensor("w1", [D, FF], F32, kind="ExternalInput")
    b1_d = nc.dram_tensor("b1", [FF], F32, kind="ExternalInput")
    w2_d = nc.dram_tensor("w2", [FF, D], F32, kind="ExternalInput")
    b2_d = nc.dram_tensor("b2", [D], F32, kind="ExternalInput")
    y_d = nc.dram_tensor("y", [S, N, D], F32, kind="ExternalOutput")
    spk = "ExternalOutput" if debug_spills else "Internal"
    xnsp_d = nc.dram_tensor("xn_sp", [S, 128, 8, N], BF16, kind=spk)
    qsp_d = nc.dram_tensor("q_sp", [S, 128, 8, N], BF16, kind=spk)
    p1sp_d = nc.dram_tensor("p1_sp", [S, 16, 128, D], F32, kind=spk)

    ident_d = nc.inline_tensor(
        np.ascontiguousarray(np.eye(128, dtype=np.float32).astype(BF)), name="identu")
    sel_np = np.zeros((16, 8 * 128), dtype=np.float32)
    for hp in range(8):
        sel_np[2 * hp, hp * 128: hp * 128 + 64] = 1.0
        sel_np[2 * hp + 1, hp * 128 + 64: hp * 128 + 128] = 1.0
    sel_d = nc.inline_tensor(np.ascontiguousarray(sel_np.astype(BF)), name="selu")
    ones_d = nc.inline_tensor(
        np.ascontiguousarray(np.ones((1, 512), np.float32).astype(BF)), name="onesu")
    CONSTS = [0.125, EXPB, 256.0, 0.25, 1.0]
    consts_d = nc.inline_tensor(np.tile(np.array(CONSTS, np.float32)[None, :], (128, 1)), name="constsf")

    SBUF_LO = 16 * 1024 + 512
    SBUF_HI = 229344 - 64
    cur = [SBUF_LO]

    def sa(name, shape, dtype):
        per = int(np.prod(shape[1:])) * mybir.dt.size(dtype)
        at = (cur[0] + 31) & ~31
        cur[0] = at + per
        assert cur[0] <= SBUF_HI, f"SBUF overflow at {name}: {cur[0]}"
        return nc.alloc_sbuf_tensor_at(name, list(shape), dtype, offset=at)

    consts = sa("consts", [128, len(CONSTS)], F32)
    ident = sa("ident", [128, 128], BF16)
    sel = sa("sel", [16, 8 * 128], BF16)
    ones_bf = sa("ones_bf", [1, 512], BF16)
    b1c = sa("b1c", [128, FC], F32)
    b2_bf = sa("b2_bf", [1, D], BF16)
    bproj_bf = sa("bproj_bf", [1, D], BF16)
    b1st = sa("b1st", [128, FC], F32)
    ident32 = sa("ident32", [128, 128], F32)
    PERSIST_END = cur[0]

    for i, v in enumerate(CONSTS):
        nc.const_aps.register(F32, v, consts.ap()[:, i:i + 1])

    def region(start):
        saved = [start]

        def alloc(name, shape, dtype):
            cur[0] = saved[0]
            t = sa(name, shape, dtype)
            saved[0] = cur[0]
            return t
        return alloc, saved

    saAB, endAB = region(PERSIST_END)
    K_sb = saAB("K_sb", [128, 8 * N], BF16)
    vt = saAB("vt", [128, VROW], BF16)
    AB_END = endAB[0]

    saA, endA = region(AB_END)
    wqkvT = saA("wqkvT", [128, 8 * 3072], BF16)
    xn_fm = saA("xn_fm", [128, 8 * N], BF16)
    xbuf = saA("xbuf", [128, 4 * D], F32)
    qstage = saA("qstage", [128, 8 * 512], BF16)
    xnstage = saA("xnstage", [128, 8 * 512], BF16)
    vstage = saA("vstage", [128, 2 * 512], BF16)
    xn_tm = saA("xn_tm", [128, 2 * D], BF16)
    st12 = saA("st12", [128, 4 * 12], F32)
    mv8 = saA("mv8", [128, 8], F32)
    lnb = saA("lnb", [128, 8 * 4], F32)

    saB, endB = region(AB_END)
    wprojT = saB("wprojT", [128, 8 * D], BF16)
    qblk = saB("qblk", [128, 2 * 8 * QB], BF16)
    pT = saB("pT", [128, 2 * KC * QB], BF16)
    attn_un = saB("attn_un", [128, 8 * QB], BF16)
    attn_fm = saB("attn_fm", [128, 8 * QB], BF16)
    ptmp = saB("ptmp", [64, 2 * QB], BF16)
    denstage = saB("denstage", [128, 2 * QB], F32)
    den16 = saB("den16", [16, QB], F32)
    rden_bf = saB("rden_bf", [16, QB], BF16)
    rdbc = saB("rdbc", [128, QB], F32)
    p1stage = saB("p1stage", [128, D], F32)
    mpart = saB("mpart", [128, 64], F32)
    m128 = saB("m128", [128, 16], F32)
    mneg = saB("mneg", [128, 2 * 4 * 65], F32)
    mrow2 = saB("mrow2", [65, 2 * 16 * QB], BF16)
    onesp = saB("onesp", [65, 128], BF16)

    saD, endD = region(PERSIST_END)
    w1_sb = saD("w1_sb", [128, 8 * FF], BF16)
    w2_sb = saD("w2_sb", [128, FC * D], BF16)
    xnblk = saD("xnblk", [128, 2 * 8 * TB], BF16)
    h2 = saD("h2", [128, 3 * 512], BF16)
    g_ki = saD("g_ki", [128, 2 * 512], I32)
    g_v = saD("g_v", [128, 512], F32)
    g_ji = saD("g_ji", [128, 512], I32)
    g_g = saD("g_g", [128, 2 * 512], F32)
    g_yj = saD("g_yj", [128, 2 * 512], F32)
    g_yj1 = saD("g_yj1", [128, 2 * 512], F32)
    g_f = saD("g_f", [128, 2 * 512], F32)
    g_d = saD("g_d", [128, 512], F32)
    g_y = saD("g_y", [128, 512], F32)
    g_yi = saD("g_yi", [128, 2 * 512], I32)
    yi2 = saD("yi2", [128, 2 * 512], I32)
    xfin = saD("xfin", [128, 2 * D], F32)
    p1fin = saD("p1fin", [128, 2 * D], F32)
    outstage = saD("outstage", [128, D], F32)
    assert max(endA[0], endB[0], endD[0]) <= SBUF_HI, (endA[0], endB[0], endD[0])
    bstage2 = nc.alloc_sbuf_tensor_at("bstage2", [1, 2 * D], F32, offset=(PERSIST_END + 31) & ~31)

    ENGS = ("sync", "tensor", "scalar", "vector", "gpsimd")
    prog = {e: [] for e in ENGS}
    cnt = {"ld": 0, "st": 0, "pe": 0, "act": 0, "dve": 0}
    hwm = {e: {s: 0 for s in cnt} for e in ENGS}
    sems = {}

    def wait(e, sem, val):
        if val is None or val <= 0 or hwm[e][sem] >= val:
            return
        hwm[e][sem] = val
        prog[e].append(lambda eng, s=sem, v=val: eng.wait_ge(sems[s], v))

    def ins(e, fn, sem=None, n=1):
        if sem is None:
            prog[e].append(fn)
            return None
        cnt[sem] += n
        v = cnt[sem]
        prog[e].append(lambda eng, f=fn, s=sem, k=n: f(eng).then_inc(sems[s], k))
        return v

    def barrier():
        snap = dict(cnt)
        for e in ENGS:
            for s, v in snap.items():
                wait(e, s, v)

    def setup():
        ins("sync", lambda e: e.dma_start(consts.ap()[:, :], consts_d.ap()[:, :]), "ld", 16)
        ins("sync", lambda e: e.dma_start(ident.ap()[:, :], ident_d.ap()[:, :]), "ld", 16)
        ins("sync", lambda e: e.dma_start(sel.ap()[:, :], sel_d.ap()[:, :]), "ld", 16)
        ins("sync", lambda e: e.dma_start(ones_bf.ap()[:, :], ones_d.ap()[:, :]), "ld", 16)
        ins("sync", lambda e: e.dma_start(b1st.ap()[:, :], b1_d.ap().rearrange("(a b) -> b a", b=128),
                                          allow_slow_non_contiguous=True), "ld", 16)
        ins("sync", lambda e: e.dma_start(bstage2.ap()[0:1, 0:D], b2_d.ap().rearrange("(o a) -> o a", o=1)), "ld", 16)
        v = ins("sync", lambda e: e.dma_start(bstage2.ap()[0:1, D:2 * D], bproj_d.ap().rearrange("(o a) -> o a", o=1)), "ld", 16)
        wait("vector", "ld", v)
        ins("vector", lambda e: e.tensor_scalar(b1c.ap()[:, :], b1st.ap()[:, :], 256.0, None, OP.mult), "dve")
        ins("vector", lambda e: e.tensor_copy(ident32.ap()[:, :], ident.ap()[:, :]), "dve")
        ins("vector", lambda e: e.tensor_copy(b2_bf.ap()[0:1, :], bstage2.ap()[0:1, 0:D]), "dve")
        return ins("vector", lambda e: e.tensor_copy(bproj_bf.ap()[0:1, :], bstage2.ap()[0:1, D:2 * D]), "dve")

    def prep_wT(nchunks, src_d, stg, bf, dst, dst_stride, ps_tr, nbuf=2):
        stcast, trdone, evdone = {}, {}, {}
        dview = dst.ap().rearrange("p (dc q) -> p dc q", dc=8)
        pview = ps_tr.ap().rearrange("p (dc t) -> p dc t", dc=8)
        for c in range(nchunks):
            b = c % nbuf
            if c >= nbuf:
                wait("sync", "dve", stcast[b])
            v = ins("sync", lambda e, c=c, b=b: e.dma_start(
                stg.ap()[:, b * D:(b + 1) * D], src_d.ap()[c * 128:(c + 1) * 128, :]), "ld", 16)
            wait("vector", "ld", v)
            if c >= nbuf:
                wait("vector", "pe", trdone[b])
            stcast[b] = ins("vector", lambda e, b=b: e.tensor_copy(
                bf.ap()[:, b * D:(b + 1) * D], stg.ap()[:, b * D:(b + 1) * D]), "dve")
            wait("tensor", "dve", stcast[b])
            if c >= 1:
                wait("tensor", "dve", evdone[(c - 1) % nbuf])
            for dc in range(8):
                ins("tensor", lambda e, b=b, dc=dc: e.transpose(
                    ps_tr.ap()[:, dc * 128:(dc + 1) * 128],
                    bf.ap()[:, b * D + dc * 128: b * D + (dc + 1) * 128],
                    ident.ap()[:, :]), None)
            trdone[b] = ins("tensor", lambda e: e.drain(), "pe")
            wait("vector", "pe", trdone[b])
            evdone[b] = ins("vector", lambda e, c=c: e.tensor_copy(
                dview[:, :, c * 128:(c + 1) * 128], pview), "dve")
        return cnt["dve"]

    def phase_A(s, wq_ready, ps_qkv, ps_tr, ps_vt):
        xn_ready = {}
        xlast = {}
        xn_t_done = {}
        xnspill = {}
        qev, kvev = {}, {}
        vtr, vev = {}, {}
        qspill_st = None
        xn_fm_v = xn_fm.ap().rearrange("p (dc t) -> p dc t", dc=8)
        xns_v = xnstage.ap().rearrange("p (dc t) -> p dc t", dc=8)
        ptr_v = ps_tr.ap().rearrange("p (dc t) -> p dc t", dc=8)
        mvv = mv8.ap().rearrange("p (c two) -> p c two", two=2)
        for tb in range(NB):
            for c in range(4):
                tc = 4 * tb + c
                if tb > 0:
                    wait("sync", "act", xlast[tc - 4])
                v = ins("sync", lambda e, s=s, tc=tc, c=c: e.dma_start(
                    xbuf.ap()[:, c * D:(c + 1) * D],
                    x_d.ap()[s, tc * 128:(tc + 1) * 128, :]), "ld", 16)
                wait("vector", "ld", v)
                if tb > 0 and c == 0:
                    wait("vector", "act", xlast[tc - 1])
                o = c * 12
                ins("vector", lambda e, c=c, o=o: e.bn_stats(st12.ap()[:, o:o + 6], xbuf.ap()[:, c * D:c * D + 512]))
                ins("vector", lambda e, c=c, o=o: e.bn_stats(st12.ap()[:, o + 6:o + 12], xbuf.ap()[:, c * D + 512:c * D + 1024]))
            for c in range(4):
                ins("vector", lambda e, c=c: e.bn_aggr(mv8.ap()[:, 2 * c:2 * c + 2], st12.ap()[:, 12 * c:12 * c + 12]))
            ins("vector", lambda e: e.drain())
            sv = ins("vector", lambda e: e.tensor_scalar(lnb.ap()[:, 0:4], mvv[:, :, 1], EPS, None, OP.add), "dve")
            wait("scalar", "dve", sv)
            av = ins("scalar", lambda e: e.activation(lnb.ap()[:, 4:8], lnb.ap()[:, 0:4], AF.Sqrt), "act")
            wait("vector", "act", av)
            ins("vector", lambda e: e.reciprocal(lnb.ap()[:, 8:12], lnb.ap()[:, 4:8]))
            ins("vector", lambda e: e.drain())
            ins("vector", lambda e: e.tensor_mul(lnb.ap()[:, 12:16], lnb.ap()[:, 0:4], lnb.ap()[:, 8:12]))
            ins("vector", lambda e: e.drain())
            ins("vector", lambda e: e.tensor_mul(lnb.ap()[:, 16:20], lnb.ap()[:, 12:16], lnb.ap()[:, 8:12]))
            ins("vector", lambda e: e.drain())
            ins("vector", lambda e: e.tensor_scalar(lnb.ap()[:, 20:24], lnb.ap()[:, 16:20], -0.5, 1.5, OP.mult, OP.add))
            ins("vector", lambda e: e.drain())
            ins("vector", lambda e: e.tensor_mul(lnb.ap()[:, 24:28], lnb.ap()[:, 8:12], lnb.ap()[:, 20:24]))
            ins("vector", lambda e: e.drain())
            nv = ins("vector", lambda e: e.scalar_tensor_tensor(
                lnb.ap()[:, 28:32], mvv[:, :, 0], -1.0, lnb.ap()[:, 24:28], OP.mult, OP.mult), "dve")
            wait("scalar", "dve", nv)
            for c in range(4):
                tc = 4 * tb + c
                par = tc % 2
                wait("scalar", "pe", xn_t_done.get(par))
                xlast[tc] = ins("scalar", lambda e, c=c, par=par: e.activation(
                    xn_tm.ap()[:, par * D:(par + 1) * D], xbuf.ap()[:, c * D:(c + 1) * D],
                    AF.Identity, bias=lnb.ap()[:, 28 + c:29 + c], scale=lnb.ap()[:, 24 + c:25 + c]), "act")
                wait("tensor", "act", xlast[tc])
                wait("tensor", "st" if tc == 0 else "dve", wq_ready if tc == 0 else xn_ready[tc - 1])
                for dc in range(8):
                    ins("tensor", lambda e, par=par, dc=dc: e.transpose(
                        ps_tr.ap()[:, dc * 128:(dc + 1) * 128],
                        xn_tm.ap()[:, par * D + dc * 128: par * D + (dc + 1) * 128],
                        ident.ap()[:, :]), None)
                xn_t_done[par] = ins("tensor", lambda e: e.drain(), "pe")
                wait("vector", "pe", xn_t_done[par])
                if tb > 0 and c == 0:
                    wait("vector", "st", xnspill[tb - 1])
                ins("vector", lambda e, tc=tc: e.tensor_copy(xn_fm_v[:, :, tc * 128:(tc + 1) * 128], ptr_v))
                xn_ready[tc] = ins("vector", lambda e, c=c: e.tensor_copy(
                    xns_v[:, :, c * 128:(c + 1) * 128], ptr_v), "dve")
            wait("gpsimd", "dve", xn_ready[4 * tb + 3])
            xnspill[tb] = ins("gpsimd", lambda e, s=s, tb=tb: e.dma_start(
                xnsp_d.ap()[s, :, :, tb * 512:(tb + 1) * 512], xns_v), "st", 16)

            wait("tensor", "dve", xn_ready[4 * tb + 3])
            if tb == 0:
                wait("tensor", "st", wq_ready)
            for fc24 in range(24):
                bank = fc24 % 3
                prev = fc24 - 3 if fc24 >= 3 else (fc24 + 21 if tb > 0 else None)
                if prev is not None:
                    if prev < 8:
                        wait("tensor", "act", qev[prev])
                    else:
                        wait("tensor", "dve", kvev[prev])
                for dc in range(8):
                    ins("tensor", lambda e, fc24=fc24, dc=dc, tb=tb, bank=bank: e.matmul(
                        ps_qkv[bank].ap()[:, :],
                        wqkvT.ap()[:, dc * 3072 + fc24 * 128: dc * 3072 + (fc24 + 1) * 128],
                        xn_fm.ap()[:, dc * 2048 + tb * 512: dc * 2048 + (tb + 1) * 512],
                        start=(dc == 0), stop=(dc == 7)), None)
                mm = ins("tensor", lambda e: e.drain(), "pe")
                if fc24 < 8:
                    wait("scalar", "pe", mm)
                    if tb > 0 and fc24 == 0:
                        wait("scalar", "st", qspill_st)
                    qev[fc24] = ins("scalar", lambda e, fc24=fc24, bank=bank: e.activation(
                        qstage.ap()[:, fc24 * 512:(fc24 + 1) * 512],
                        ps_qkv[bank].ap()[:, :], AF.Copy, scale=0.125), "act")
                elif fc24 < 16:
                    hp = fc24 - 8
                    wait("vector", "pe", mm)
                    kvev[fc24] = ins("vector", lambda e, hp=hp, tb=tb, bank=bank: e.tensor_copy(
                        K_sb.ap()[:, hp * 2048 + tb * 512: hp * 2048 + (tb + 1) * 512],
                        ps_qkv[bank].ap()[:, :]), "dve")
                else:
                    hp = fc24 - 16
                    vb = hp % 2
                    wait("vector", "pe", mm)
                    wait("vector", "pe", vtr.get(vb))
                    kvev[fc24] = ins("vector", lambda e, vb=vb, bank=bank: e.tensor_copy(
                        vstage.ap()[:, vb * 512:(vb + 1) * 512], ps_qkv[bank].ap()[:, :]), "dve")
                    wait("tensor", "dve", kvev[fc24])
                    wait("tensor", "dve", vev.get(1 - vb))
                    for j in range(4):
                        ins("tensor", lambda e, vb=vb, j=j: e.transpose(
                            ps_vt.ap()[:, j * 128:(j + 1) * 128],
                            vstage.ap()[:, vb * 512 + j * 128: vb * 512 + (j + 1) * 128],
                            ident.ap()[:, :]), None)
                    vtr[vb] = ins("tensor", lambda e: e.drain(), "pe")
                    wait("vector", "pe", vtr[vb])
                    vev[vb] = ins("vector", lambda e, tb=tb, hp=hp: e.tensor_copy(
                        bass.AP(vt, (4 * tb) * 1040 + hp * 130, [[VROW, 128], [1040, 4], [65, 2], [1, 64]]),
                        bass.AP(ps_vt, 0, [[512, 128], [128, 4], [64, 2], [1, 64]])), "dve")
            wait("gpsimd", "act", qev[7])
            qspill_st = ins("gpsimd", lambda e, s=s, tb=tb: e.dma_start(
                qsp_d.ap()[s, :, :, tb * 512:(tb + 1) * 512],
                qstage.ap().rearrange("p (qc t) -> p qc t", qc=8)), "st", 16)
        ins("vector", lambda e: e.memset(bass.AP(vt, 64, [[VROW, 128], [130, 128], [1, 1]]), 1.0), "dve")
        ov = ins("vector", lambda e: e.memset(bass.AP(vt, 129, [[VROW, 128], [130, 128], [1, 1]]), 1.0), "dve")
        return ov, qspill_st, xnspill[NB - 1]

    def phase_B(s, vt_ready, qsp_ready, ps_sx, ps_tm, ps_rd, ps_proj, ps_m):
        qblk_v = qblk.ap().rearrange("p (b qc t) -> p b qc t", b=2, qc=8)
        qld = {}
        exp_prev = pv_done = pv_evac = pv_evac_se = rdevac = pevac = pspill = None
        pspill2 = {}
        normmul = {}
        mxfer = {}
        tmz_last = {}
        tmred = {}
        tmred_last = [None]
        exp_hist, gkc = {}, [0]
        tmb = [ps_tm[0], ps_tm[1], ps_rd, ps_m]
        ins("vector", lambda e: e.memset(onesp.ap()[:, :], 1.0), "dve")

        def tm_mm(qbn, hp, m, pe_guard):
            kc, par = m // 2, m % 2
            q128, kgr = kc // 4, kc % 4
            qbb, bank = qbn % 2, tmb[m % 4]
            p0 = 64 * par
            wait("tensor", "dve", tmred.get((qbn, hp, m - 4)))
            if m % 4 == 2:
                wait("tensor", "dve", rdevac)
            if m % 4 == 3:
                wait("tensor", "dve", tmz_last.get("evac"))
            ins("tensor", lambda e, hp=hp, q128=q128, kgr=kgr, p0=p0, qbb=qbb, bank=bank: e.matmul(
                bank.ap()[:, :],
                qblk.ap()[p0:p0 + 64, qbb * (8 * QB) + hp * QB + q128 * 128: qbb * (8 * QB) + hp * QB + q128 * 128 + 128],
                K_sb.ap()[p0:p0 + 64, hp * 2048 + kgr * 512: hp * 2048 + (kgr + 1) * 512],
                start=True, stop=True, tile_position=(p0, 0)), None)

        def tm_red(qbn, hp, m, pe_guard):
            kc, par = m // 2, m % 2
            q128, kgr = kc // 4, kc % 4
            par_q = qbn % 2
            bank = tmb[m % 4]
            col = par * 16 + q128 * 4 + kgr
            wait("vector", "pe", pe_guard)
            r = ins("vector", lambda e, col=col, par_q=par_q, bank=bank: e.reduce_max(
                mpart.ap()[:, par_q * 32 + col:par_q * 32 + col + 1],
                bank.ap()[:, :], mybir.AxisListType.X), "dve")
            tmred[(qbn, hp, m)] = r
            tmred_last[0] = r

        def tm_combine(qbn, hp):
            par_q = qbn % 2
            ins("vector", lambda e: e.drain())
            for par in range(2):
                for q128 in range(4):
                    o = par_q * 32 + par * 16 + q128 * 4
                    ins("vector", lambda e, o=o, par_q=par_q, par=par, q128=q128: e.reduce_max(
                        m128.ap()[:, par_q * 8 + 4 * par + q128: par_q * 8 + 4 * par + q128 + 1],
                        mpart.ap()[:, o:o + 4], mybir.AxisListType.X))
            ins("vector", lambda e: e.drain())
            mz = ins("vector", lambda e, par_q=par_q: e.tensor_scalar(
                bass.AP(mneg, par_q * (4 * 65), [[2 * 4 * 65, 128], [64, 2], [65, 4]]),
                bass.AP(m128, par_q * 8, [[16, 128], [4, 2], [1, 4]]),
                -1.0, None, OP.mult), "dve")
            wait("tensor", "dve", mz)
            wait("tensor", "dve", tmz_last.get("evac"))
            for q128 in range(4):
                ins("tensor", lambda e, q128=q128, par_q=par_q: e.transpose(
                    ps_m.ap()[0:65, q128 * 128:(q128 + 1) * 128],
                    mneg.ap()[:, par_q * (4 * 65) + 65 * q128: par_q * (4 * 65) + 65 * q128 + 65],
                    ident32.ap()[:, :]), None)
            tmz = ins("tensor", lambda e: e.drain(), "pe")
            wait("vector", "pe", tmz)
            ev = None
            for q128 in range(4):
                for par in range(2):
                    h = 2 * hp + par
                    ev = ins("vector", lambda e, par=par, h=h, q128=q128, par_q=par_q: e.tensor_copy(
                        mrow2.ap()[64 * par:64 * par + 1, par_q * (16 * QB) + h * QB + q128 * 128: par_q * (16 * QB) + h * QB + q128 * 128 + 128],
                        ps_m.ap()[64 * par:64 * par + 1, q128 * 128:(q128 + 1) * 128]), "dve")
            mxfer[(qbn, hp)] = ev
            tmz_last["evac"] = ev

        wait("sync", "st", qsp_ready)
        for pf in range(2):
            qld[pf] = ins("sync", lambda e, s=s, pf=pf: e.dma_start(
                qblk_v[:, pf], qsp_d.ap()[s, :, :, pf * QB:(pf + 1) * QB]), "ld", 16)
        wait("tensor", "ld", qld[0])
        wait("tensor", "dve", vt_ready)
        for hp in range(8):
            for kc in range(KC):
                for j in range(2):
                    tm_mm(0, hp, 2 * kc + j, None)
                dr = ins("tensor", lambda e: e.drain(), "pe")
                for j in range(2):
                    tm_red(0, hp, 2 * kc + j, dr)
            tm_combine(0, hp)

        for qb in range(NQB):
            qbb = qb % 2
            if 1 <= qb < NQB - 1:
                wait("sync", "pe", scores_last_of_prev)
                qld[(qb + 1) % 2] = ins("sync", lambda e, s=s, qb=qb: e.dma_start(
                    qblk_v[:, (qb + 1) % 2], qsp_d.ap()[s, :, :, (qb + 1) * QB:(qb + 2) * QB]), "ld", 16)
            wait("tensor", "ld", qld[qbb])
            den_dma, move_dma = {}, {}
            for hp in range(8):
                exp_done = None
                scores_done = None
                wait("tensor", "dve", mxfer[(qb, hp)])
                if qb + 1 < NQB and hp == 0:
                    wait("tensor", "ld", qld[(qb + 1) % 2])
                for kcp in range(KC // 2):
                    for kc in (2 * kcp, 2 * kcp + 1):
                        sb = ps_sx if kc % 2 == 0 else ps_proj
                        wait("tensor", "act", exp_hist.get(gkc[0] + (kc % 2) - 2))
                        if kc == 1:
                            wait("tensor", "dve", pv_evac)
                            wait("tensor", "act", pv_evac_se)
                            if qb > 0 and hp == 0:
                                wait("tensor", "dve", pevac)
                        for par in range(2):
                            p0 = 64 * par
                            ins("tensor", lambda e, hp=hp, kc=kc, par=par, p0=p0, qbb=qbb, sb=sb: e.matmul(
                                sb.ap()[:, par * 512:(par + 1) * 512],
                                K_sb.ap()[p0:p0 + 64, hp * 2048 + kc * 128: hp * 2048 + (kc + 1) * 128],
                                qblk.ap()[p0:p0 + 64, qbb * (8 * QB) + hp * QB: qbb * (8 * QB) + (hp + 1) * QB],
                                start=True, stop=False, tile_position=(p0, 0)),
                                None)
                        for par in range(2):
                            h = 2 * hp + par
                            ins("tensor", lambda e, par=par, h=h, qbb=qbb, sb=sb: e.matmul(
                                sb.ap()[:, par * 512:(par + 1) * 512],
                                onesp.ap()[64 * par:64 * par + 1, 0:128],
                                mrow2.ap()[64 * par:64 * par + 1, qbb * (16 * QB) + h * QB: qbb * (16 * QB) + (h + 1) * QB],
                                start=False, stop=True, tile_position=(64 * par, 0)),
                                None)
                        if qb + 1 < NQB:
                            for j in range(2):
                                tm_mm(qb + 1, hp, 2 * kc + j, None)
                    scores_done = ins("tensor", lambda e: e.drain(), "pe")
                    if qb + 1 < NQB:
                        for kk in (2 * kcp, 2 * kcp + 1):
                            for j in range(2):
                                tm_red(qb + 1, hp, 2 * kk + j, scores_done)
                    wait("scalar", "pe", scores_done)
                    if kcp == 0:
                        wait("scalar", "pe", pv_done)
                    for kc in (2 * kcp, 2 * kcp + 1):
                        sb = ps_sx if kc % 2 == 0 else ps_proj
                        ev = ins("scalar", lambda e, kc=kc, sb=sb: e.activation(
                            bass.AP(pT, kc * QB, [[2 * KC * QB, 128], [KC * QB, 2], [1, QB]]),
                            sb.ap()[:, :], AF.Exp), "act")
                        exp_hist[gkc[0]] = ev
                        gkc[0] += 1
                        exp_prev = ev
                        exp_done = ev
                scores_last_of_prev = scores_done
                if qb + 1 < NQB:
                    tm_combine(qb + 1, hp)
                wait("tensor", "act", exp_done)
                wait("tensor", "dve", pv_evac)
                wait("tensor", "act", pv_evac_se)
                if qb > 0 and hp == 0:
                    wait("tensor", "dve", pevac)
                for par in range(2):
                    for kc in range(KC):
                        ins("tensor", lambda e, hp=hp, par=par, kc=kc: e.matmul(
                            ps_proj.ap()[0:65, par * 512:(par + 1) * 512],
                            vt.ap()[:, kc * 1040 + hp * 130 + par * 65: kc * 1040 + hp * 130 + par * 65 + 65],
                            pT.ap()[:, par * (KC * QB) + kc * QB: par * (KC * QB) + (kc + 1) * QB],
                            start=(kc == 0), stop=(kc == KC - 1)),
                            None)
                pv_done = ins("tensor", lambda e: e.drain(), "pe")
                wait("vector", "pe", pv_done)
                ins("vector", lambda e: e.tensor_copy(
                    denstage.ap()[64:65, 0:QB], ps_proj.ap()[64:65, 0:512]))
                dnv = ins("vector", lambda e: e.tensor_copy(
                    denstage.ap()[64:65, QB:2 * QB], ps_proj.ap()[64:65, 512:1024]), "dve")
                wait("scalar", "pe", pv_done)
                if hp >= 2:
                    wait("scalar", "ld", move_dma[hp - 2])
                ins("scalar", lambda e, hp=hp: e.activation(
                    attn_un.ap()[0:64, hp * QB:(hp + 1) * QB], ps_proj.ap()[0:64, 0:512], AF.Copy))
                pv_evac_se = ins("scalar", lambda e, hp=hp: e.activation(
                    ptmp.ap()[0:64, (hp % 2) * QB:(hp % 2 + 1) * QB], ps_proj.ap()[0:64, 512:1024], AF.Copy), "act")
                pv_evac = dnv
                wait("sync", "dve", dnv)
                den_dma[hp] = ins("sync", lambda e, hp=hp: e.dma_start(
                    den16.ap()[2 * hp:2 * hp + 2, 0:QB], denstage.ap()[64:65, 0:2 * QB]), "ld", 16)
                wait("sync", "act", pv_evac_se)
                move_dma[hp] = ins("sync", lambda e, hp=hp: e.dma_start(
                    attn_un.ap()[64:128, hp * QB:(hp + 1) * QB],
                    ptmp.ap()[0:64, (hp % 2) * QB:(hp % 2 + 1) * QB]), "ld", 16)
                wait("vector", "ld", den_dma[hp])
            wait("vector", "ld", den_dma[7])
            ins("vector", lambda e: e.reciprocal(den16.ap()[:, :], den16.ap()[:, :]))
            ins("vector", lambda e: e.drain())
            cv = ins("vector", lambda e: e.tensor_copy(rden_bf.ap()[:, :], den16.ap()[:, :]), "dve")
            wait("tensor", "dve", cv)
            wait("tensor", "dve", tmred_last[0])
            for hp in range(8):
                wait("tensor", "dve", rdevac)
                ins("tensor", lambda e, hp=hp: e.matmul(
                    ps_rd.ap()[:, :], sel.ap()[:, hp * 128:(hp + 1) * 128], rden_bf.ap()[:, :],
                    start=True, stop=True), None)
                bmm = ins("tensor", lambda e: e.drain(), "pe")
                wait("vector", "pe", bmm)
                rdevac = ins("vector", lambda e: e.tensor_copy(rdbc.ap()[:, :], ps_rd.ap()[:, :]), "dve")
                ins("vector", lambda e: e.drain())
                wait("vector", "ld", move_dma[hp])
                normmul[hp] = ins("vector", lambda e, hp=hp: e.tensor_mul(
                    attn_fm.ap()[:, hp * QB:(hp + 1) * QB],
                    attn_un.ap()[:, hp * QB:(hp + 1) * QB], rdbc.ap()[:, :]), "dve")
            wait("tensor", "dve", normmul[7])
            for th in range(4):
                wait("tensor", "dve", pevac)
                for half in range(2):
                    ins("tensor", lambda e, half=half: e.matmul(
                        ps_proj.ap()[:, half * 512:(half + 1) * 512],
                        ones_bf.ap()[0:1, 0:128], bproj_bf.ap()[0:1, half * 512:(half + 1) * 512],
                        start=True, stop=False))
                    for dc in range(8):
                        ins("tensor", lambda e, half=half, dc=dc, th=th: e.matmul(
                            ps_proj.ap()[:, half * 512:(half + 1) * 512],
                            attn_fm.ap()[:, dc * QB + th * 128: dc * QB + th * 128 + 128],
                            wprojT.ap()[:, dc * D + half * 512: dc * D + (half + 1) * 512],
                            start=False, stop=(dc == 7)), None)
                pmm = ins("tensor", lambda e: e.drain(), "pe")
                wait("vector", "pe", pmm)
                wait("vector", "st", pspill)
                pevac = ins("vector", lambda e, th=th: e.tensor_copy(
                    p1stage.ap()[:, :], ps_proj.ap()[:, :]), "dve")
                wait("gpsimd", "dve", pevac)
                pspill = ins("gpsimd", lambda e, s=s, qb=qb, th=th: e.dma_start(
                    p1sp_d.ap()[s, 4 * qb + th], p1stage.ap()[:, :]), "st", 16)
        return pspill

    def phase_D(s, xnsp_ready, p1_ready, ps_c1, ps_c2):
        xnb_v = xnblk.ap().rearrange("p (b dc t) -> p b dc t", b=2, dc=8)
        wait("sync", "st", xnsp_ready)
        wait("sync", "st", p1_ready)

        w1q = {}
        w2c = {}
        wjobs = [("w1", q, 2 * q) for q in range(8)] + \
                [("w2", c, c // 2 + 4) for c in range(FC)]
        wjobs.sort(key=lambda j: j[2])
        for kind, idx, _need in wjobs:
            if kind == "w1":
                for c in range(8):
                    m = ins("gpsimd", lambda e, c=c, q=idx: e.dma_start(
                        w1_sb.ap()[:, c * FF + q * 512: c * FF + (q + 1) * 512],
                        w1_d.ap()[c * 128:(c + 1) * 128, q * 512:(q + 1) * 512]), "st", 16)
                w1q[idx] = m
            else:
                w2c[idx] = ins("gpsimd", lambda e, c=idx: e.dma_start(
                    w2_sb.ap()[:, c * D:(c + 1) * D],
                    w2_d.ap()[c * 128:(c + 1) * 128, :]), "st", 16)

        xnld = {}
        xnld[0] = ins("sync", lambda e, s=s: e.dma_start(xnb_v[:, 0], xnsp_d.ap()[s, :, :, 0:TB]), "ld", 16)
        xnld[1] = ins("sync", lambda e, s=s: e.dma_start(xnb_v[:, 1], xnsp_d.ap()[s, :, :, TB:2 * TB]), "ld", 16)

        NF = NTB * 16
        c1_reader, h2_reader = {}, {}
        s2_end, s4_end, hv_m, ki_m, yj1_m, ykv_m = {}, {}, {}, {}, {}, {}
        c1d_m, c2closed, xfm, pfm = {}, {}, {}, {}
        finadd = ki2v = ysent = None

        def ybuf(par):
            return (g_y if par == 0 else g_d).ap()

        for f in range(NF + 4):
            tb, fcp, gb = f // 16, f % 16, f % 2
            fC2 = f - 4
            if f < NF:
                if fcp == 0:
                    wait("tensor", "ld", xnld[tb % 2])
                wait("tensor", "act", c1_reader.get(gb))
                if tb == 0:
                    wait("tensor", "st", w1q[min((2 * fcp + 1) // 4 + 2, 7)])
                for i in range(2):
                    fc = 2 * fcp + i
                    for dc in range(8):
                        ins("tensor", lambda e, fc=fc, i=i, dc=dc, tbb=tb % 2, gb=gb: e.matmul(
                            ps_c1[gb].ap()[:, i * TB:(i + 1) * TB],
                            w1_sb.ap()[:, dc * FF + fc * 128: dc * FF + (fc + 1) * 128],
                            xnblk.ap()[:, tbb * (8 * TB) + dc * TB: tbb * (8 * TB) + (dc + 1) * TB],
                            start=(dc == 0), stop=(dc == 7)), None)
                c1d_m[f] = ins("tensor", lambda e: e.drain(), "pe")
            if 0 <= fC2 < NF:
                tb2, fcp2, sl2 = fC2 // 16, fC2 % 16, fC2 % 3
                wait("tensor", "dve", hv_m[fC2])
                if tb2 == 0:
                    wait("tensor", "st", w2c[min(2 * fcp2 + 8, FC - 1)])
                if fcp2 == 0:
                    wait("tensor", "act", ki2v)
                    for th in range(2):
                        for half in range(2):
                            ins("tensor", lambda e, th=th, half=half: e.matmul(
                                ps_c2[th].ap()[:, half * 512:(half + 1) * 512],
                                ones_bf.ap()[0:1, 0:128], b2_bf.ap()[0:1, half * 512:(half + 1) * 512],
                                start=True, stop=False))
                m = None
                for i in range(2):
                    fc = 2 * fcp2 + i
                    for th in range(2):
                        for half in range(2):
                            m = ins("tensor", lambda e, fc=fc, i=i, th=th, half=half, sl2=sl2: e.matmul(
                                ps_c2[th].ap()[:, half * 512:(half + 1) * 512],
                                h2.ap()[:, sl2 * 512 + i * TB + th * 128: sl2 * 512 + i * TB + th * 128 + 128],
                                w2_sb.ap()[:, fc * D + half * 512: fc * D + (half + 1) * 512],
                                start=False, stop=(fc == FC - 1)),
                                "pe" if (i, th, half) == (1, 1, 1) else None)
                h2_reader[sl2] = m
                if fcp2 == 15:
                    c2closed[tb2] = ins("tensor", lambda e: e.drain(), "pe")
            if f < NF:
                wait("scalar", "pe", c1d_m[f])
                wait("scalar", "dve", s2_end.get(f - 2))
                kv = None
                for i in range(2):
                    fc = 2 * fcp + i
                    kv = ins("scalar", lambda e, i=i, fc=fc, gb=gb: e.activation(
                        g_ki.ap()[:, gb * 512 + i * TB: gb * 512 + (i + 1) * TB],
                        ps_c1[gb].ap()[:, i * TB:(i + 1) * TB],
                        AF.Identity, bias=b1c.ap()[:, fc:fc + 1], scale=256.0), "act" if i == 1 else None)
                ki_m[f] = kv
                c1_reader[gb] = kv
            fy = f - 2
            if 0 <= fy < NF:
                gby = fy % 2
                wait("scalar", "dve", s4_end[fy])
                ykv_m[fy] = ins("scalar", lambda e, gby=gby, yb=ybuf(fy % 2): e.activation(
                    g_yi.ap()[:, gby * 512:(gby + 1) * 512], yb[:, :],
                    AF.Copy, scale=256.0), "act")
            fj = f - 1
            if 0 <= fj < NF:
                gbj = fj % 2
                wait("scalar", "dve", s2_end[fj])
                ins("scalar", lambda e, gbj=gbj: e.activation(
                    g_yj.ap()[:, gbj * 512:(gbj + 1) * 512],
                    g_g.ap()[:, gbj * 512:(gbj + 1) * 512], AF.Gelu, scale=0.25), None)
                yj1_m[fj] = ins("scalar", lambda e, gbj=gbj: e.activation(
                    g_yj1.ap()[:, gbj * 512:(gbj + 1) * 512],
                    g_g.ap()[:, gbj * 512:(gbj + 1) * 512], AF.Gelu, bias=0.25, scale=0.25), "act")
            if 0 <= fy < NF:
                gby = fy % 2
                sly = fy % 3
                wait("vector", "act", ykv_m[fy])
                wait("vector", "pe", h2_reader.get(sly))
                ins("vector", lambda e, gby=gby, sly=sly: e.tensor_scalar(
                    h2.ap()[:, sly * 512:(sly + 1) * 512], g_yi.ap()[:, gby * 512:(gby + 1) * 512],
                    1.0 / 256.0, None, OP.mult))
                hv_m[fy] = ins("vector", lambda e: e.drain(), "dve")
            if f < NF:
                wait("vector", "act", ki_m[f])
                ins("vector", lambda e, gb=gb: e.tensor_scalar(
                    g_v.ap()[:, :], g_ki.ap()[:, gb * 512:(gb + 1) * 512],
                    1.0 / 64.0, 15.5, OP.mult, OP.add))
                ins("vector", lambda e: e.tensor_copy(g_ji.ap()[:, :], g_v.ap()[:, :]))
                ins("vector", lambda e, gb=gb: e.tensor_scalar(
                    g_g.ap()[:, gb * 512:(gb + 1) * 512], g_ji.ap()[:, :],
                    127.0, 16.0, OP.min, OP.subtract))
                s2_end[f] = ins("vector", lambda e, gb=gb: e.scalar_tensor_tensor(
                    g_f.ap()[:, gb * 512:(gb + 1) * 512], g_v.ap()[:, :], 15.5,
                    g_g.ap()[:, gb * 512:(gb + 1) * 512], OP.subtract, OP.subtract), "dve")
            if 0 <= fj < NF:
                gbj = fj % 2
                wait("vector", "act", yj1_m[fj])
                yb = ybuf(gbj)
                ins("vector", lambda e, yb=yb, gbj=gbj: e.tensor_sub(
                    yb[:, :], g_yj1.ap()[:, gbj * 512:(gbj + 1) * 512],
                    g_yj.ap()[:, gbj * 512:(gbj + 1) * 512]))
                ins("vector", lambda e, yb=yb, gbj=gbj: e.tensor_mul(
                    yb[:, :], yb[:, :], g_f.ap()[:, gbj * 512:(gbj + 1) * 512]))
                s4_end[fj] = ins("vector", lambda e, yb=yb, gbj=gbj: e.tensor_add(
                    yb[:, :], yb[:, :], g_yj.ap()[:, gbj * 512:(gbj + 1) * 512]), "dve")
            if f < NF and fcp == 0 and 1 <= tb < NTB - 1:
                wait("sync", "pe", c1d_m[16 * tb - 1])
                xnld[(tb + 1) % 2] = ins("sync", lambda e, s=s, tb=tb: e.dma_start(
                    xnb_v[:, (tb + 1) % 2], xnsp_d.ap()[s, :, :, (tb + 1) * TB:(tb + 2) * TB]), "ld", 16)
            if f < NF and fcp == 4:
                wait("sync", "dve", finadd)
                xfm[tb] = ins("sync", lambda e, s=s, tb=tb: e.dma_start(
                    xfin.ap().rearrange("p (tc d) -> p tc d", tc=2),
                    x_d.ap()[s, 2 * tb * 128:(2 * tb + 2) * 128, :].rearrange("(tc p) d -> p tc d", p=128)), "ld", 16)
                pfm[tb] = ins("sync", lambda e, s=s, tb=tb: e.dma_start(
                    p1fin.ap().rearrange("p (tc d) -> p tc d", tc=2),
                    p1sp_d.ap()[s, 2 * tb:2 * tb + 2].rearrange("tc p d -> p tc d")), "ld", 16)
            if 0 <= fC2 < NF and fC2 % 16 == 15:
                tb2 = fC2 // 16
                wait("scalar", "pe", c2closed[tb2])
                for th in range(2):
                    wait("scalar", "dve", finadd)
                    ki2v = ins("scalar", lambda e, th=th: e.activation(
                        yi2.ap()[:, :], ps_c2[th].ap()[:, :], AF.Copy, scale=256.0), "act")
                    wait("vector", "act", ki2v)
                    wait("vector", "ld", xfm[tb2])
                    wait("vector", "ld", pfm[tb2])
                    wait("vector", "st", ysent)
                    ins("vector", lambda e, th=th: e.scalar_tensor_tensor(
                        outstage.ap()[:, :], yi2.ap()[:, :], 1.0 / 256.0,
                        xfin.ap()[:, th * D:(th + 1) * D], OP.mult, OP.add))
                    finadd = ins("vector", lambda e, th=th: e.tensor_add(
                        outstage.ap()[:, :], outstage.ap()[:, :],
                        p1fin.ap()[:, th * D:(th + 1) * D]), "dve")
                    wait("gpsimd", "dve", finadd)
                    ysent = ins("gpsimd", lambda e, s=s, tb2=tb2, th=th: e.dma_start(
                        y_d.ap()[s, (2 * tb2 + th) * 128:(2 * tb2 + th + 1) * 128, :],
                        outstage.ap()[:, :]), "st", 16)
        return cnt["st"]

    setup_dve = setup()

    gA = [nc.psum_tensor("ps_qkv%d_0" % i, [128, 512], F32) for i in range(3)]
    ps_qkv = [g.__enter__() for g in gA]
    gtr = nc.psum_tensor("ps_tr_0", [128, 1024], BF16)
    ps_tr = gtr.__enter__()
    gvt = nc.psum_tensor("ps_vt_0", [128, 512], BF16)
    ps_vt = gvt.__enter__()

    for s in range(S):
        barrier()
        for dc in range(8):
            wq_ready = ins("gpsimd", lambda e, dc=dc: e.dma_start(
                wqkvT.ap()[:, dc * 3072:(dc + 1) * 3072],
                wqkv_d.ap()[dc * 128:(dc + 1) * 128, :]), "st", 16)
        wait("vector", "dve", setup_dve)
        vt_ready, qsp_ready, xnsp_ready = phase_A(s, wq_ready, ps_qkv, ps_tr, ps_vt)
        barrier()
        for dc in range(8):
            wp_ready = ins("gpsimd", lambda e, dc=dc: e.dma_start(
                wprojT.ap()[:, dc * D:(dc + 1) * D],
                wproj_d.ap()[dc * 128:(dc + 1) * 128, :]), "st", 16)
        gvt.__exit__(None, None, None)
        gtr.__exit__(None, None, None)
        for g in reversed(gA):
            g.__exit__(None, None, None)
        gB = [nc.psum_tensor("ps_sx_%d" % s, [128, 1024], F32),
              nc.psum_tensor("ps_tm0_%d" % s, [128, 512], F32),
              nc.psum_tensor("ps_tm1_%d" % s, [128, 512], F32),
              nc.psum_tensor("ps_rd_%d" % s, [128, 512], F32),
              nc.psum_tensor("ps_proj_%d" % s, [128, 1024], F32),
              nc.psum_tensor("ps_m_%d" % s, [128, 512], F32)]
        ps_sx, ps_tm0, ps_tm1, ps_rd, ps_proj, ps_m = [g.__enter__() for g in gB]
        wait("tensor", "st", wp_ready)
        p1_ready = phase_B(s, vt_ready, qsp_ready, ps_sx, [ps_tm0, ps_tm1], ps_rd, ps_proj, ps_m)
        barrier()
        for g in reversed(gB):
            g.__exit__(None, None, None)
        gD = [nc.psum_tensor("ps_c1a_%d" % s, [128, 512], F32),
              nc.psum_tensor("ps_c1b_%d" % s, [128, 512], F32),
              nc.psum_tensor("ps_c2a_%d" % s, [128, 1024], F32),
              nc.psum_tensor("ps_c2b_%d" % s, [128, 1024], F32)]
        ps_c1a, ps_c1b, ps_c2a, ps_c2b = [g.__enter__() for g in gD]
        phase_D(s, xnsp_ready, p1_ready, [ps_c1a, ps_c1b], [ps_c2a, ps_c2b])
        for g in reversed(gD):
            g.__exit__(None, None, None)
        if s + 1 < S:
            barrier()
            gA = [nc.psum_tensor("ps_qkv%d_%d" % (i, s + 1), [128, 512], F32) for i in range(3)]
            ps_qkv = [g.__enter__() for g in gA]
            gtr = nc.psum_tensor("ps_tr_%d" % (s + 1), [128, 1024], BF16)
            ps_tr = gtr.__enter__()
            gvt = nc.psum_tensor("ps_vt_%d" % (s + 1), [128, 512], BF16)
            ps_vt = gvt.__enter__()

    with (
        nc.semaphore("ld") as sem_ld,
        nc.semaphore("st") as sem_st,
        nc.semaphore("pe") as sem_pe,
        nc.semaphore("act") as sem_act,
        nc.semaphore("dve") as sem_dve,
        nc.Block() as block,
    ):
        sems.update(ld=sem_ld, st=sem_st, pe=sem_pe, act=sem_act, dve=sem_dve)

        @block.sync
        def _(eng):
            for f in prog["sync"]:
                f(eng)

        @block.tensor
        def _(eng):
            for f in prog["tensor"]:
                f(eng)

        @block.scalar
        def _(eng):
            for f in prog["scalar"]:
                f(eng)

        @block.vector
        def _(eng):
            for f in prog["vector"]:
                f(eng)

        @block.gpsimd
        def _(eng):
            for f in prog["gpsimd"]:
                f(eng)
            eng.wait_ge(sem_st, cnt["st"])

    return nc


_cached = {}


def kernel(**inputs):
    x = np.ascontiguousarray(np.asarray(inputs["L0_attn_x"], dtype=np.float32))
    wqkv = np.ascontiguousarray(np.asarray(inputs["L0_attn_w_qkv"], dtype=np.float32))
    wproj = np.ascontiguousarray(np.asarray(inputs["L0_attn_w_proj"], dtype=np.float32))
    bproj = np.ascontiguousarray(np.asarray(inputs["L0_attn_b_proj"], dtype=np.float32))
    w1 = np.ascontiguousarray(np.asarray(inputs["L0_ffn_w1"], dtype=np.float32))
    b1 = np.ascontiguousarray(np.asarray(inputs["L0_ffn_b1"], dtype=np.float32))
    w2 = np.ascontiguousarray(np.asarray(inputs["L0_ffn_w2"], dtype=np.float32))
    b2 = np.ascontiguousarray(np.asarray(inputs["L0_ffn_b2"], dtype=np.float32))

    if "nc" not in _cached:
        _cached["nc"] = build_program()
    nc = _cached["nc"]
    in_maps = [{
        "x": np.ascontiguousarray(x[2 * c:2 * c + 2]),
        "w_qkvT": np.ascontiguousarray(wqkv.T), "w_projT": np.ascontiguousarray(wproj.T), "b_proj": bproj,
        "w1": w1, "b1": b1, "w2": w2, "b2": b2,
    } for c in range(8)]
    res = run_bass_kernel_spmd(nc, in_maps, core_ids=list(range(8)))
    return np.concatenate([r["y"] for r in res.results], axis=0)
